# version 07
# baseline (speedup 1.0000x reference)
import contextlib
import numpy as np
import ml_dtypes

import concourse.bass as bass
import concourse.mybir as mybir
from concourse.bass_utils import run_bass_kernel_spmd

F32 = mybir.dt.float32
BF16 = mybir.dt.bfloat16
AF = mybir.ActivationFunctionType
ALU = mybir.AluOpType
AX = mybir.AxisListType

B_FULL, L, D = 16, 2048, 1024
N_CORES = 8
SEQ_PER_CORE = B_FULL // N_CORES
H = 8
HD = 128
E_R, E_S, NEXP = 8, 2, 10
HID = 2048
INV_SQRT_HD = 1.0 / float(np.sqrt(HD))
EPS = 1e-5

ENGINES = ("t", "v", "s", "y", "g")


SEMS = ("t", "v", "s", "gc", "gp0", "gp1", "gp2", "gs0", "gs1", "gx", "go",
        "y0", "y1", "yo", "ys0", "ys1", "yw0", "yw1")


class Builder:

    def __init__(self):
        self.ops = {e: [] for e in ENGINES}
        self.cnt = {s: 0 for s in SEMS}
        self.waited = {e: {s: 0 for s in SEMS} for e in ENGINES}

    def do(self, e, fn, waits=(), inc=None):
        w = []
        for (se, pv) in waits:
            if pv is None or pv <= 0:
                continue
            assert pv <= self.cnt[se], (
                f"wait on future count: {e} waits {se}>={pv}, have {self.cnt[se]}")
            if pv > self.waited[e][se]:
                self.waited[e][se] = pv
                w.append((se, pv))
        if isinstance(inc, int):
            inc = (e, inc) if inc else None
        self.ops[e].append((w, fn, inc))
        if inc is not None:
            self.cnt[inc[0]] += inc[1]
            return self.cnt[inc[0]]
        return None

    def drain(self, e):
        self.ops[e].append(([], lambda eng: eng.drain(), None))


def build_nc(nranges=4):
    nc = bass.Bass("TRN2", target_bir_lowering=False)
    bl = Builder()

    x_in = nc.dram_tensor("x", [SEQ_PER_CORE, L, D], F32, kind="ExternalInput").ap()
    ipw = nc.dram_tensor("in_proj_w", [3 * D, D], F32, kind="ExternalInput").ap()
    ipb = nc.dram_tensor("in_proj_b", [3 * D], F32, kind="ExternalInput").ap()
    outw = nc.dram_tensor("out_w", [D, D], F32, kind="ExternalInput").ap()
    outb = nc.dram_tensor("out_b", [D], F32, kind="ExternalInput").ap()
    rw = nc.dram_tensor("router_w", [D, E_R], F32, kind="ExternalInput").ap()
    rb = nc.dram_tensor("router_b", [E_R], F32, kind="ExternalInput").ap()
    w1 = nc.dram_tensor("w1", [E_R, D, HID], F32, kind="ExternalInput").ap()
    b1 = nc.dram_tensor("b1", [E_R, HID], F32, kind="ExternalInput").ap()
    w2 = nc.dram_tensor("w2", [E_R, HID, D], F32, kind="ExternalInput").ap()
    b2 = nc.dram_tensor("b2", [E_R, D], F32, kind="ExternalInput").ap()
    sw1 = nc.dram_tensor("sw1", [E_S, D, HID], F32, kind="ExternalInput").ap()
    sb1 = nc.dram_tensor("sb1", [E_S, HID], F32, kind="ExternalInput").ap()
    sw2 = nc.dram_tensor("sw2", [E_S, HID, D], F32, kind="ExternalInput").ap()
    sb2 = nc.dram_tensor("sb2", [E_S, D], F32, kind="ExternalInput").ap()
    identc = nc.dram_tensor("identc", [128, 128], F32, kind="ExternalInput").ap()
    selc = nc.dram_tensor("selc", [16, 16 * 128], BF16, kind="ExternalInput").ap()
    iotac = nc.dram_tensor("iotac", [128, E_R], F32, kind="ExternalInput").ap()
    y_out = nc.dram_tensor("yo", [SEQ_PER_CORE, L, D], F32, kind="ExternalOutput").ap()

    qkvT = nc.dram_tensor("qkvT", [24, 128, 2, 8, 128], BF16)
    woT = nc.dram_tensor("woT", [8, 128, 2, D], BF16)
    w1s = nc.dram_tensor("w1s", [NEXP, 16, 128, 2, 8, 128], BF16)
    w2s = nc.dram_tensor("w2s", [NEXP, 16, 128, 2, D], BF16)

    NB = 66560
    NF = 16384
    ab = nc.alloc_sbuf_tensor("ab", [128, NB], BF16)
    af = nc.alloc_sbuf_tensor("af", [128, NF], F32)

    O_XLN_H, O_XLN_L = 0, 8192
    O_Q_H, O_Q_L = 16384, 20480
    O_K_H, O_K_L = 24576, 28672
    O_OT_H = 16384
    O_V_H, O_V_L = 32768, 40960
    O_O_H, O_O_L = 49152, 53248
    O_H_H, O_H_L = 16384, 32768
    O_WS = 57344
    O_AT_H, O_AT_L = 65536, 66048

    O_X1 = 0
    O_V32 = 8192
    O_QK32 = 12288
    O_SC32 = 13312
    O_G32 = 8192
    O_STAT = 14864
    O_GATE = 14912
    O_COMB = 14976
    O_T2 = 15040
    O_STG = 15104
    O_JUNK = 15104
    LD_A, LD_B = 4096, 2048

    ident = nc.alloc_sbuf_tensor("ident", [128, 128], F32)
    sel = nc.alloc_sbuf_tensor("sel", [16, 16 * 128], BF16)
    iota = nc.alloc_sbuf_tensor("iota", [128, E_R], F32)
    inb = nc.alloc_sbuf_tensor("inb", [128, 24], F32)
    b1s = nc.alloc_sbuf_tensor("b1s", [128, NEXP * 16], F32)
    brow = nc.alloc_sbuf_tensor("brow", [16, 2048], BF16)
    rwt = nc.alloc_sbuf_tensor("rwt", [128, 8, 2, E_R], BF16)
    stat = nc.alloc_sbuf_tensor("stat", [128, 48], F32)
    ps = [nc.alloc_psum_tensor(f"ps{i}", [128, 512], F32) for i in range(8)]

    es = contextlib.ExitStack()
    sem = {s: es.enter_context(nc.semaphore(f"sem_{s}")) for s in SEMS}

    bank_free = {i: [] for i in range(8)}

    def A(off, n):
        return ab[:, off:off + n]

    def Af(off, n):
        return af[:, off:off + n]

    def mm3(t, out_ap, lh, ll, rh, rl, start, stop):
        t.matmul(out_ap, lh, rh, start=start, stop=False)
        t.matmul(out_ap, lh, rl, start=False, stop=False)
        return t.matmul(out_ap, ll, rh, start=False, stop=stop)

    def sel_bias(t, out_ap, row, col0, n, stop):
        t.matmul(out_ap, sel[:, row * 128:(row + 1) * 128],
                 brow[:, col0:col0 + n], start=False, stop=False)
        return t.matmul(out_ap, sel[:, row * 128:(row + 1) * 128],
                        brow[:, 1024 + col0:1024 + col0 + n], start=False, stop=stop)

    bl.do("g", lambda g: g.dma_start(out=ident[:], in_=identc[:, :]), inc=("gc", 16))
    bl.do("g", lambda g: g.dma_start(out=sel[:], in_=selc[:, :]), inc=("gc", 16))
    bl.do("g", lambda g: g.dma_start(out=iota[:], in_=iotac[:, :]), inc=("gc", 16))
    bl.do("g", lambda g: g.dma_start(
        out=inb[:], in_=ipb.rearrange("(f p) -> p f", p=128),
        allow_slow_non_contiguous=True), inc=("gc", 16))
    for e in range(E_R):
        bl.do("g", lambda g, e=e: g.dma_start(
            out=b1s[:, e * 16:(e + 1) * 16],
            in_=b1[e].rearrange("(h p) -> p h", p=128),
            allow_slow_non_contiguous=True), inc=("gc", 16))
    for e in range(E_S):
        bl.do("g", lambda g, e=e: g.dma_start(
            out=b1s[:, (E_R + e) * 16:(E_R + e + 1) * 16],
            in_=sb1[e].rearrange("(h p) -> p h", p=128),
            allow_slow_non_contiguous=True), inc=("gc", 16))
    bl.do("v", lambda v: v.memset(af[0:1, O_T2 + 56:O_T2 + 57], 0.0), inc=0)
    c_vz = bl.do("v", lambda v: v.memset(af[0:16, 0:1024], 0.0), inc=1)
    bl.do("g", lambda g: g.dma_start(out=af[0:8, 0:1024], in_=b2[:, :]),
          waits=[("v", c_vz)], inc=("gc", 16))
    bl.do("g", lambda g: g.dma_start(
        out=af[8:9, 0:1024], in_=outb.rearrange("(o n) -> o n", o=1)), inc=("gc", 16))
    bl.do("g", lambda g: g.dma_start(
        out=af[9:10, 0:1024], in_=sb2[0:1, :]), inc=("gc", 16))
    bl.do("g", lambda g: g.dma_start(
        out=af[12:13, 0:1024], in_=sb2[1:2, :]), inc=("gc", 16))
    bl.do("g", lambda g: g.dma_start(
        out=af[10:11, 0:E_R], in_=rb.rearrange("(o n) -> o n", o=1)), inc=("gc", 16))
    GC = bl.do("g", lambda g: g.dma_start(
        out=af[:, 6144:6208].rearrange("p (dc e) -> p dc e", dc=8),
        in_=rw.rearrange("(dc p) e -> p dc e", p=128)), inc=("gc", 16))
    bl.do("v", lambda v: v.tensor_copy(brow[0:16, 0:1024], af[0:16, 0:1024]),
          waits=[("gc", GC)], inc=0)
    bl.drain("v")
    bl.do("v", lambda v: v.tensor_tensor(
        brow[0:16, 1024:2048], af[0:16, 0:1024], brow[0:16, 0:1024], ALU.subtract),
        inc=1)
    bl.do("v", lambda v: v.tensor_copy(
        rwt[:, :, 0, :], af[:, 6144:6208].rearrange("p (dc e) -> p dc e", dc=8)),
        inc=0)
    bl.drain("v")
    bl.do("v", lambda v: v.tensor_tensor(
        rwt[:, :, 1, :], af[:, 6144:6208].rearrange("p (dc e) -> p dc e", dc=8),
        rwt[:, :, 0, :], ALU.subtract), inc=1)

    ld_par = [0]
    ld_cons = {0: [], 1: [], 2: []}
    LD_OFF = {0: 4096, 1: 2048, 2: 6144}

    def prep_load(in_ap_fn, n=1024):
        par = ld_par[0]
        ld_par[0] = (par + 1) % 3
        off = LD_OFF[par]
        c = bl.do("g", lambda g, off=off, n=n: g.dma_start(
            out=af[:, off:off + n], in_=in_ap_fn()),
            waits=ld_cons[par], inc=("gp%d" % par, 16))
        return par, off, c

    qk_store_cnt = [0, 0]
    for fc in range(24):
        par, off, c_ld = prep_load(lambda fc=fc: ipw[fc * 128:(fc + 1) * 128, :])
        sp = fc % 2
        stg0 = sp * 2048
        for dc in range(8):
            bank = ps[dc % 4]
            w = [("gp%d" % par, c_ld), ("gs%d" % sp, qk_store_cnt[sp])] + bank_free[dc % 4]
            if fc == 0 and dc == 0:
                w += [("gc", GC)]
            c_t = bl.do("t", lambda t, dc=dc, bank=bank, off=off: t.transpose(
                bank[:, 0:128], af[:, off + dc * 128:off + (dc + 1) * 128],
                ident[:]), waits=w, inc=1)
            c_s = bl.do("s", lambda s, dc=dc, bank=bank, stg0=stg0: s.activation(
                ab[:, stg0 + dc * 128:stg0 + (dc + 1) * 128], bank[:, 0:128],
                AF.Copy), waits=[("t", c_t)], inc=1)
            c_v = bl.do("v", lambda v, dc=dc, bank=bank, stg0=stg0: v.tensor_tensor(
                ab[:, stg0 + 1024 + dc * 128:stg0 + 1024 + (dc + 1) * 128],
                bank[:, 0:128], ab[:, stg0 + dc * 128:stg0 + (dc + 1) * 128],
                ALU.subtract), waits=[("s", c_s)], inc=1)
            bank_free[dc % 4] = [("v", c_v)]
        ld_cons[par] = [("t", bl.cnt["t"])]
        qk_store_cnt[sp] = bl.do("g", lambda g, fc=fc, stg0=stg0: g.dma_start(
            out=qkvT.ap()[fc].rearrange("p s dc c -> p (s dc c)"),
            in_=ab[:, stg0:stg0 + 2048]), waits=[("v", bl.cnt["v"])],
            inc=("gs%d" % sp, 16))

    for fo in range(8):
        par, off, c_ld = prep_load(lambda fo=fo: outw[fo * 128:(fo + 1) * 128, :])
        for fi in range(8):
            bank = ps[fi % 4]
            w = [("gp%d" % par, c_ld)] + bank_free[fi % 4]
            if fo == 0 and fi < 2:
                w += [("gs0", qk_store_cnt[0]), ("gs1", qk_store_cnt[1])]
            c_t = bl.do("t", lambda t, fi=fi, bank=bank, off=off: t.transpose(
                bank[:, 0:128], af[:, off + fi * 128:off + (fi + 1) * 128],
                ident[:]), waits=w, inc=1)
            o_h = 4096 + fi * 2048 + fo * 128
            o_l = 4096 + fi * 2048 + 1024 + fo * 128
            c_s = bl.do("s", lambda s, bank=bank, o_h=o_h: s.activation(
                ab[:, o_h:o_h + 128], bank[:, 0:128], AF.Copy),
                waits=[("t", c_t)], inc=1)
            c_v = bl.do("v", lambda v, bank=bank, o_h=o_h, o_l=o_l: v.tensor_tensor(
                ab[:, o_l:o_l + 128], bank[:, 0:128], ab[:, o_h:o_h + 128],
                ALU.subtract), waits=[("s", c_s)], inc=1)
            bank_free[fi % 4] = [("v", c_v)]
        ld_cons[par] = [("t", bl.cnt["t"])]
    c_v = bl.cnt["v"]
    wo_store_cnt = qk_store_cnt[0]
    for fi in range(8):
        wo_store_cnt = bl.do("g", lambda g, fi=fi: g.dma_start(
            out=woT.ap()[fi].rearrange("p s f -> p (s f)"),
            in_=ab[:, 4096 + fi * 2048:4096 + (fi + 1) * 2048]),
            waits=[("v", c_v)], inc=("gs0", 16))
    GS0, GS1 = bl.cnt["gs0"], bl.cnt["gs1"]

    ys_cnt = [0, 0]
    yw_cnt = [0, 0]
    for e in range(NEXP):
        src1 = w1 if e < E_R else sw1
        src2 = w2 if e < E_R else sw2
        ei = e if e < E_R else e - E_R
        ep = e % 2
        hi_stg = ab[:, 0:16384].rearrange("p (hc dc c) -> p hc dc c", dc=8, c=128)
        lo_stg = ab[:, 16384:32768].rearrange("p (hc dc c) -> p hc dc c", dc=8, c=128)
        for dc in range(8):
            par, off, c_ld = prep_load(
                lambda dc=dc, src1=src1, ei=ei: src1[ei, dc * 128:(dc + 1) * 128, :],
                n=2048)
            w = [("gp%d" % par, c_ld), ("ys%d" % (1 - ep), ys_cnt[1 - ep])]
            if e == 0 and dc == 0:
                w += [("gs0", GS0), ("gs1", GS1)]
            in_v = af[:, off:off + 2048].rearrange("p (hc c) -> p hc c", c=128)
            c_s = bl.do("s", lambda s, dc=dc, in_v=in_v: s.activation(
                hi_stg[:, :, dc, :], in_v, AF.Copy), waits=w, inc=1)
            c_v = bl.do("v", lambda v, dc=dc, in_v=in_v: v.tensor_tensor(
                lo_stg[:, :, dc, :], in_v, hi_stg[:, :, dc, :], ALU.subtract),
                waits=[("s", c_s)], inc=1)
            ld_cons[par] = [("v", c_v)]
        c_v = bl.cnt["v"]
        for hc in range(16):
            bl.do("y", lambda y, e=e, hc=hc: y.dma_start(
                out=w1s.ap()[e, hc, :, 0].rearrange("p dc c -> p (dc c)"),
                in_=ab[:, hc * 1024:(hc + 1) * 1024]),
                waits=[("v", c_v)], inc=("ys%d" % ep, 16))
            ys_cnt[ep] = bl.do("y", lambda y, e=e, hc=hc: y.dma_start(
                out=w1s.ap()[e, hc, :, 1].rearrange("p dc c -> p (dc c)"),
                in_=ab[:, 16384 + hc * 1024:16384 + (hc + 1) * 1024]),
                inc=("ys%d" % ep, 16))
        for hc in range(16):
            par, off, c_ld = prep_load(
                lambda hc=hc, src2=src2, ei=ei: src2[ei, hc * 128:(hc + 1) * 128, :])
            wp = hc % 2
            wst = 32768 + wp * 2048
            c_s = bl.do("s", lambda s, off=off, wst=wst: s.activation(
                ab[:, wst:wst + 1024], af[:, off:off + 1024], AF.Copy),
                waits=[("gp%d" % par, c_ld), ("yw%d" % wp, yw_cnt[wp])], inc=1)
            c_v2 = bl.do("v", lambda v, off=off, wst=wst: v.tensor_tensor(
                ab[:, wst + 1024:wst + 2048], af[:, off:off + 1024],
                ab[:, wst:wst + 1024], ALU.subtract), waits=[("s", c_s)], inc=1)
            ld_cons[par] = [("v", c_v2)]
            yw_cnt[wp] = bl.do("y", lambda y, e=e, hc=hc, wst=wst: y.dma_start(
                out=w2s.ap()[e, hc].rearrange("p s f -> p (s f)"),
                in_=ab[:, wst:wst + 2048]), waits=[("v", c_v2)],
                inc=("yw%d" % wp, 16))

    PREP_YS0, PREP_YS1 = bl.cnt["ys0"], bl.cnt["ys1"]
    PREP_YW0, PREP_YW1 = bl.cnt["yw0"], bl.cnt["yw1"]
    bank_free = {i: [("v", bl.cnt["v"]), ("s", bl.cnt["s"])] for i in range(8)}

    ws_free = [[], []]
    last_store_g = 0
    x1_free = [("v", bl.cnt["v"]), ("ys0", PREP_YS0), ("ys1", PREP_YS1),
               ("yw0", PREP_YW0), ("yw1", PREP_YW1), ("gs0", GS0), ("gs1", GS1)]

    def ws_slot(par):
        return O_WS + par * 4096

    def ln_stats(tc0, ntc, x_ready):
        for i in range(ntc):
            tc = tc0 + i
            x = Af(O_X1 + tc * 1024, 1024)
            bl.do("v", lambda v, x=x, tc=tc: v.reduce_sum(
                stat[:, tc:tc + 1], x, axis=AX.X), waits=x_ready, inc=0)
            bl.do("v", lambda v, x=x, tc=tc: v.scalar_tensor_tensor(
                Af(O_JUNK, 1024), x, 1.0, x, ALU.mult, ALU.mult,
                accum_out=stat[:, 8 + tc:8 + tc + 1]), inc=0)
            bl.drain("v")
        a, b = tc0, tc0 + ntc
        bl.do("v", lambda v: v.tensor_scalar(
            stat[:, 16 + a:16 + b], stat[:, a:b], 1.0 / 1024.0, None, ALU.mult), inc=0)
        bl.do("v", lambda v: v.tensor_scalar(
            stat[:, 24 + a:24 + b], stat[:, 8 + a:8 + b], 1.0 / 1024.0, None, ALU.mult), inc=0)
        bl.drain("v")
        bl.do("v", lambda v: v.tensor_mul(
            stat[:, 40 + a:40 + b], stat[:, 16 + a:16 + b], stat[:, 16 + a:16 + b]), inc=0)
        bl.drain("v")
        bl.do("v", lambda v: v.tensor_tensor(
            stat[:, 24 + a:24 + b], stat[:, 24 + a:24 + b], stat[:, 40 + a:40 + b],
            ALU.subtract), inc=0)
        bl.drain("v")
        c_v = bl.do("v", lambda v: v.tensor_scalar(
            stat[:, 24 + a:24 + b], stat[:, 24 + a:24 + b], EPS, None, ALU.add), inc=1)
        bl.do("s", lambda s: s.activation(
            stat[:, 40 + a:40 + b], stat[:, 24 + a:24 + b], AF.Ln),
            waits=[("v", c_v)], inc=1)
        c_v = bl.do("v", lambda v: v.tensor_scalar(
            stat[:, 40 + a:40 + b], stat[:, 40 + a:40 + b], -0.5, None, ALU.mult),
            waits=[("s", bl.cnt["s"])], inc=1)
        return bl.do("s", lambda s: s.activation(
            stat[:, 32 + a:32 + b], stat[:, 40 + a:40 + b], AF.Exp),
            waits=[("v", c_v)], inc=1)

    def ln_apply(tc0, ntc, c_s_rs):
        ln_last_t = {0: 0, 1: 0}
        for i in range(ntc):
            tc = tc0 + i
            for fg in range(2):
                it = i * 2 + fg
                par = it % 2
                stg = Af(O_STG + par * 512, 512)
                bank = ps[par]
                c_v0 = bl.do("v", lambda v, tc=tc, fg=fg, stg=stg: v.tensor_scalar(
                    stg, Af(O_X1 + tc * 1024 + fg * 512, 512),
                    stat[:, 16 + tc:16 + tc + 1], stat[:, 32 + tc:32 + tc + 1],
                    ALU.subtract, ALU.mult),
                    waits=[("s", c_s_rs), ("t", ln_last_t[par])], inc=1)
                for j in range(4):
                    c_t = bl.do("t", lambda t, stg=stg, bank=bank, j=j: t.transpose(
                        bank[:, j * 128:(j + 1) * 128],
                        stg[:, j * 128:(j + 1) * 128], ident[:]),
                        waits=([("v", c_v0)] + bank_free[par]) if j == 0 else [],
                        inc=1)
                ln_last_t[par] = bl.cnt["t"]
                col = tc * 128
                fc0 = fg * 4
                hi_view = A(O_XLN_H, 8192).rearrange(
                    "p (f c) -> p f c", c=1024)[:, fc0:fc0 + 4, col:col + 128]
                lo_view = A(O_XLN_L, 8192).rearrange(
                    "p (f c) -> p f c", c=1024)[:, fc0:fc0 + 4, col:col + 128]
                bank_view = bank[:, 0:512].rearrange("p (f c) -> p f c", c=128)
                c_s = bl.do("s", lambda s, bank_view=bank_view, hi_view=hi_view:
                            s.activation(hi_view, bank_view, AF.Copy),
                            waits=[("t", c_t)], inc=1)
                c_v = bl.do("v", lambda v, bank_view=bank_view, hi_view=hi_view,
                            lo_view=lo_view: v.tensor_tensor(
                    lo_view, bank_view, hi_view, ALU.subtract),
                    waits=[("s", c_s)], inc=1)
                bank_free[par] = [("v", c_v)]

    def evac_split(bank, bias_col, scale, o_h, o_l, c_t, par, n=512):
        stg = Af(O_QK32 + par * 512, n)
        if scale is not None:
            c_v0 = bl.do("v", lambda v: v.tensor_scalar(
                stg, bank[:, 0:n], bias_col, scale, ALU.add, ALU.mult),
                waits=[("t", c_t)], inc=1)
        else:
            c_v0 = bl.do("v", lambda v: v.tensor_scalar(
                stg, bank[:, 0:n], bias_col, None, ALU.add),
                waits=[("t", c_t)], inc=1)
        c_s = bl.do("s", lambda s: s.activation(A(o_h, n), stg, AF.Copy),
                    waits=[("v", c_v0)], inc=1)
        c_v = bl.do("v", lambda v: v.tensor_tensor(
            A(o_l, n), stg, A(o_h, n), ALU.subtract), waits=[("s", c_s)], inc=1)
        return c_v, c_v0

    ranges = [(s, h) for s in range(SEQ_PER_CORE) for h in range(2)][:nranges]
    for ri, (sq, hf) in enumerate(ranges):
        w = list(x1_free) + [("go", last_store_g)]
        if hf == 0:
            c_g_x = bl.do("g", lambda g, sq=sq: g.dma_start(
                out=Af(O_X1, 8192).rearrange("p (c f) -> p c f", f=1024),
                in_=x_in[sq, 32:1056, :].rearrange("(c p) f -> p c f", p=128)),
                waits=w, inc=("gx", 16))
        else:
            bl.do("g", lambda g, sq=sq: g.dma_start(
                out=Af(O_X1, 7168).rearrange("p (c f) -> p c f", f=1024),
                in_=x_in[sq, 1056:1952, :].rearrange("(c p) f -> p c f", p=128)),
                waits=w, inc=("gx", 16))
            bl.do("g", lambda g, sq=sq: g.dma_start(
                out=af[0:96, O_X1 + 7168:O_X1 + 8192],
                in_=x_in[sq, 1952:2048, :]), inc=("gx", 16))
            c_g_x = bl.do("g", lambda g, sq=sq: g.dma_start(
                out=af[96:128, O_X1 + 7168:O_X1 + 8192],
                in_=x_in[sq, 0:32, :]), inc=("gx", 16))

        ln_xln = {}
        c_rs = ln_stats(0, 4, [("gx", c_g_x)])
        ln_apply(0, 4, c_rs)
        ln_xln[0] = bl.cnt["v"]
        for sub in range(2):
            t0 = sub * 512
            tc0 = sub * 4
            c_v_xln = ln_xln[sub]

            for fc in range(24):
                par = fc % 2
                c_y = bl.do("y", lambda y, fc=fc, par=par: y.dma_start(
                    out=A(ws_slot(par), 2048),
                    in_=qkvT.ap()[fc].rearrange("p s dc c -> p (s dc c)")),
                    waits=ws_free[par] + [("gs0", GS0), ("gs1", GS1)],
                    inc=("y%d" % par, 16))
                bank = ps[2 + par]
                w = [("y%d" % par, c_y), ("v", c_v_xln)] + bank_free[2 + par]
                first = True
                for dc in range(8):
                    sl = ws_slot(par)
                    lh = A(sl + dc * 128, 128)
                    ll = A(sl + 1024 + dc * 128, 128)
                    rh = A(O_XLN_H + dc * 1024 + t0, 512)
                    rl = A(O_XLN_L + dc * 1024 + t0, 512)
                    c_t = bl.do("t", lambda t, lh=lh, ll=ll, rh=rh, rl=rl,
                                first=first, last=(dc == 7), bank=bank:
                                mm3(t, bank[:], lh, ll, rh, rl, first, last),
                                waits=(w if first else []), inc=1)
                    first = False
                ws_free[par] = [("t", c_t)]
                if fc < 8:
                    c_v, c_v0 = evac_split(bank, inb[:, fc:fc + 1], INV_SQRT_HD,
                                           O_Q_H + fc * 512, O_Q_L + fc * 512, c_t, par)
                    bank_free[2 + par] = [("v", c_v0)]
                elif fc < 16:
                    f = fc - 8
                    c_v, c_v0 = evac_split(bank, inb[:, fc:fc + 1], None,
                                           O_K_H + f * 512, O_K_L + f * 512, c_t, par)
                    bank_free[2 + par] = [("v", c_v0)]
                else:
                    f = fc - 16
                    c_v0 = bl.do("v", lambda v, bank=bank, fc=fc, f=f: v.tensor_scalar(
                        Af(O_V32 + f * 512, 512), bank[:], inb[:, fc:fc + 1], None,
                        ALU.add), waits=[("t", c_t)], inc=1)
                    bank_free[2 + par] = [("v", c_v0)]
            c_v_v32 = bl.cnt["v"]

            for wi in range(8):
                for h in range(8):
                    k = wi * 8 + h
                    bank = ps[4 + k % 2]
                    c_t = bl.do("t", lambda t, wi=wi, h=h, bank=bank: t.transpose(
                        bank[0:64, 0:128],
                        Af(O_V32 + h * 512 + wi * 64, 64), ident[:]),
                        waits=[("v", c_v_v32)] + bank_free[4 + k % 2], inc=1)
                    o_h = O_V_H + wi * 1024 + h * 128
                    o_l = O_V_L + wi * 1024 + h * 128
                    c_s = bl.do("s", lambda s, bank=bank, o_h=o_h: s.activation(
                        ab[0:64, o_h:o_h + 128], bank[0:64, 0:128], AF.Copy),
                        waits=[("t", c_t)], inc=1)
                    c_v = bl.do("v", lambda v, bank=bank, o_h=o_h, o_l=o_l:
                                v.tensor_tensor(ab[0:64, o_l:o_l + 128],
                                                bank[0:64, 0:128],
                                                ab[0:64, o_h:o_h + 128], ALU.subtract),
                                waits=[("s", c_s)], inc=1)
                    bank_free[4 + k % 2] = [("v", c_v)]
            c_v_vsplit = bl.cnt["v"]

            s32 = af[0:64, O_SC32:O_SC32 + 512]
            e32 = af[0:64, O_SC32 + 512:O_SC32 + 1024]
            a32p = [af[0:64, O_SC32 + 1024:O_SC32 + 1536],
                    af[0:64, O_QK32:O_QK32 + 512]]
            mx = af[0:64, O_SC32 + 1536:O_SC32 + 1544]
            sm = af[0:64, O_SC32 + 1544:O_SC32 + 1552]
            win_state = {}

            def emit_scores(wi):
                bankS = ps[6 + wi % 2]
                w0 = [("v", c_v_vsplit)] + bank_free[6 + wi % 2]
                for h in range(8):
                    lh = A(O_Q_H + h * 512 + wi * 64, 64)
                    ll = A(O_Q_L + h * 512 + wi * 64, 64)
                    rh = A(O_K_H + h * 512 + wi * 64, 64)
                    rl = A(O_K_L + h * 512 + wi * 64, 64)
                    out = bankS[0:64, h * 64:(h + 1) * 64]
                    bl.do("t", lambda t, lh=lh, ll=ll, rh=rh, rl=rl, out=out: (
                        t.matmul(out, lh, rh, start=True, stop=False),
                        t.matmul(out, lh, rl, start=False, stop=False),
                        t.matmul(out, ll, rh, start=False, stop=True))[-1],
                        waits=(w0 if h == 0 else []), inc=1)
                return bl.cnt["t"]

            def emit_softmax(wi, c_t_sc):
                bankS = ps[6 + wi % 2]
                a32 = a32p[wi % 2]
                c_v_cp = bl.do("v", lambda v, bankS=bankS: v.tensor_copy(
                    s32, bankS[0:64, 0:512]), waits=[("t", c_t_sc)], inc=1)
                bank_free[6 + wi % 2] = [("v", c_v_cp)]
                bl.drain("v")
                bl.do("v", lambda v: v.reduce_max(
                    mx.rearrange("p (h o) -> p h o", o=1),
                    s32.rearrange("p (h c) -> p h c", c=64), axis=AX.X), inc=0)
                bl.drain("v")
                for h in range(8):
                    bl.do("v", lambda v, h=h: v.tensor_scalar(
                        s32[:, h * 64:(h + 1) * 64], s32[:, h * 64:(h + 1) * 64],
                        mx[:, h:h + 1], None, ALU.subtract), inc=0)
                bl.drain("v")
                c_v_sub = bl.do("v", lambda v: v.tensor_copy(
                    af[0:1, O_T2 + 56:O_T2 + 57], af[0:1, O_T2 + 56:O_T2 + 57]),
                    inc=1)
                c_s_exp = bl.do("s", lambda s: s.activation(e32, s32, AF.Exp),
                                waits=[("v", c_v_sub)], inc=1)
                bl.do("v", lambda v: v.reduce_sum(
                    sm.rearrange("p (h o) -> p h o", o=1),
                    e32.rearrange("p (h c) -> p h c", c=64), axis=AX.X),
                    waits=[("s", c_s_exp)], inc=0)
                bl.drain("v")
                bl.do("v", lambda v: v.reciprocal(sm, sm), inc=0)
                bl.drain("v")
                for h in range(8):
                    bl.do("v", lambda v, h=h, a32=a32: v.tensor_scalar(
                        a32[:, h * 64:(h + 1) * 64], e32[:, h * 64:(h + 1) * 64],
                        sm[:, h:h + 1], None, ALU.mult), inc=0)
                bl.drain("v")
                return bl.do("v", lambda v: v.tensor_copy(
                    af[0:1, O_T2 + 56:O_T2 + 57], af[0:1, O_T2 + 56:O_T2 + 57]),
                    inc=1)

            def emit_tail(wi, c_v_a32):
                a32 = a32p[wi % 2]
                bankT = ps[4 + wi % 2]
                wT = [("v", c_v_a32)] + bank_free[4 + wi % 2]
                for h in range(8):
                    bl.do("t", lambda t, h=h, a32=a32, bankT=bankT: t.transpose(
                        bankT[0:64, h * 64:(h + 1) * 64],
                        a32[:, h * 64:(h + 1) * 64],
                        ident[0:64, 0:64]), waits=(wT if h == 0 else []), inc=1)
                c_t_tr = bl.cnt["t"]
                c_s_at = bl.do("s", lambda s, bankT=bankT: s.activation(
                    ab[0:64, O_AT_H:O_AT_H + 512], bankT[0:64, 0:512], AF.Copy),
                    waits=[("t", c_t_tr)], inc=1)
                c_v_at = bl.do("v", lambda v, bankT=bankT: v.tensor_tensor(
                    ab[0:64, O_AT_L:O_AT_L + 512], bankT[0:64, 0:512],
                    ab[0:64, O_AT_H:O_AT_H + 512], ALU.subtract),
                    waits=[("s", c_s_at)], inc=1)
                bank_free[4 + wi % 2] = [("v", c_v_at)]
                bankO = ps[2 + wi % 2]
                wO = [("v", c_v_at)] + bank_free[2 + wi % 2]
                for h in range(8):
                    lh = ab[0:64, O_V_H + wi * 1024 + h * 128:O_V_H + wi * 1024 + (h + 1) * 128]
                    ll = ab[0:64, O_V_L + wi * 1024 + h * 128:O_V_L + wi * 1024 + (h + 1) * 128]
                    rh = ab[0:64, O_AT_H + h * 64:O_AT_H + (h + 1) * 64]
                    rl = ab[0:64, O_AT_L + h * 64:O_AT_L + (h + 1) * 64]
                    out = bankO[:, h * 64:(h + 1) * 64]
                    bl.do("t", lambda t, lh=lh, ll=ll, rh=rh, rl=rl, out=out: (
                        t.matmul(out, lh, rh, start=True, stop=False),
                        t.matmul(out, lh, rl, start=False, stop=False),
                        t.matmul(out, ll, rh, start=False, stop=True))[-1],
                        waits=(wO if h == 0 else []), inc=1)
                c_t_o = bl.cnt["t"]
                c_s_o = bl.do("s", lambda s, wi=wi, bankO=bankO: s.activation(
                    A(O_O_H, 4096).rearrange("p (f c) -> p f c", c=512)[:, :, wi * 64:wi * 64 + 64],
                    bankO[:, 0:512].rearrange("p (f c) -> p f c", c=64), AF.Copy),
                    waits=[("t", c_t_o)], inc=1)
                c_v_o = bl.do("v", lambda v, wi=wi, bankO=bankO: v.tensor_tensor(
                    A(O_O_L, 4096).rearrange("p (f c) -> p f c", c=512)[:, :, wi * 64:wi * 64 + 64],
                    bankO[:, 0:512].rearrange("p (f c) -> p f c", c=64),
                    A(O_O_H, 4096).rearrange("p (f c) -> p f c", c=512)[:, :, wi * 64:wi * 64 + 64],
                    ALU.subtract), waits=[("s", c_s_o)], inc=1)
                bank_free[2 + wi % 2] = [("v", c_v_o)]

            for wi in range(8):
                c_t_sc = emit_scores(wi)
                c_v_a32 = emit_softmax(wi, c_t_sc)
                win_state[wi] = c_v_a32
                if wi > 0:
                    emit_tail(wi - 1, win_state[wi - 1])
            emit_tail(7, win_state[7])
            c_v_owin = bl.cnt["v"]

            if sub == 0:
                c_rs = ln_stats(4, 4, [("gx", c_g_x)])
                ln_apply(4, 4, c_rs)
                ln_xln[1] = bl.cnt["v"]

            c_t_qk_done = bl.cnt["t"]
            for fi in range(8):
                bl.do("y", lambda y, fi=fi: y.dma_start(
                    out=A(O_OT_H + fi * 2048, 2048),
                    in_=woT.ap()[fi].rearrange("p s f -> p (s f)")),
                    waits=[("t", c_t_qk_done), ("gs0", GS0)], inc=("yo", 16))
            c_y_oT = bl.cnt["yo"]
            for tcl in range(4):
                tc = tc0 + tcl
                for half in range(2):
                    par = (tcl * 2 + half) % 4
                    bank = ps[2 + par]
                    w = [("yo", c_y_oT), ("v", c_v_owin)] + bank_free[2 + par]
                    first = True
                    for fi in range(8):
                        lh = A(O_O_H + fi * 512 + tcl * 128, 128)
                        ll = A(O_O_L + fi * 512 + tcl * 128, 128)
                        rh = A(O_OT_H + fi * 2048 + half * 512, 512)
                        rl = A(O_OT_H + fi * 2048 + 1024 + half * 512, 512)
                        bl.do("t", lambda t, lh=lh, ll=ll, rh=rh, rl=rl,
                              first=first, bank=bank:
                              mm3(t, bank[:], lh, ll, rh, rl, first, False),
                              waits=(w if first else []), inc=1)
                        first = False
                    c_t = bl.do("t", lambda t, bank=bank, half=half:
                                sel_bias(t, bank[:], 8, half * 512, 512, True), inc=1)
                    c_v = bl.do("v", lambda v, bank=bank, tc=tc, half=half:
                                v.scalar_tensor_tensor(
                                    Af(O_X1 + tc * 1024 + half * 512, 512), bank[:],
                                    2.0, Af(O_X1 + tc * 1024 + half * 512, 512),
                                    ALU.mult, ALU.add), waits=[("t", c_t)], inc=1)
                    bank_free[2 + par] = [("v", c_v)]

            if sub == 0:
                ln_stats(0, 4, [("v", bl.cnt["v"]), ("t", bl.cnt["t"])])

        c_rs = ln_stats(4, 4, [("v", bl.cnt["v"]), ("t", bl.cnt["t"])])
        ln_apply(0, 8, c_rs)
        c_v_xln = bl.cnt["v"]

        for tcl in range(8):
            par = tcl % 2
            bank = ps[2 + par]
            w = [("v", c_v_xln)] + bank_free[2 + par]
            first = True
            for dc in range(8):
                lh = A(O_XLN_H + dc * 1024 + tcl * 128, 128)
                ll = A(O_XLN_L + dc * 1024 + tcl * 128, 128)
                bl.do("t", lambda t, lh=lh, ll=ll, dc=dc, first=first, bank=bank:
                      mm3(t, bank[:, 0:E_R], lh, ll, rwt[:, dc, 0, :],
                          rwt[:, dc, 1, :], first, False),
                      waits=(w if first else []), inc=1)
                first = False
            c_t = bl.do("t", lambda t, bank=bank:
                        sel_bias(t, bank[:, 0:E_R], 10, 0, E_R, True), inc=1)
            c_v = bl.do("v", lambda v, bank=bank, tcl=tcl: v.tensor_copy(
                Af(O_GATE + tcl * 8, 8), bank[:, 0:E_R]), waits=[("t", c_t)], inc=1)
            bank_free[2 + par] = [("v", c_v)]
        bl.drain("v")
        T0 = O_T2
        for tcl in range(8):
            g8 = Af(O_GATE + tcl * 8, 8)
            cb8 = Af(O_COMB + tcl * 8, 8)
            m1, m2, r1, e2v, p1, p2 = (Af(T0, 1), Af(T0 + 1, 1), Af(T0 + 2, 1),
                                       Af(T0 + 3, 1), Af(T0 + 4, 1), Af(T0 + 5, 1))
            is1, s1t = Af(T0 + 8, 8), Af(T0 + 16, 8)
            mk1, mk2, g2t = Af(T0 + 24, 8), Af(T0 + 32, 8), Af(T0 + 40, 8)
            for step in [
                lambda v, a=m1, b=g8: v.reduce_max(a, b, axis=AX.X),
                lambda v, a=is1, b=g8, c=m1: v.tensor_scalar(a, b, c, None, ALU.is_ge),
                lambda v, a=s1t, b=is1: v.tensor_mul(a, b, iota[:]),
                lambda v, a=r1, b=s1t: v.reduce_max(a, b, axis=AX.X),
                lambda v, a=mk1, b=s1t, c=r1: v.tensor_scalar(a, b, c, None, ALU.is_equal),
                lambda v, a=g2t, b=mk1, c=g8: v.scalar_tensor_tensor(
                    a, b, -1e30, c, ALU.mult, ALU.add),
                lambda v, a=m2, b=g2t: v.reduce_max(a, b, axis=AX.X),
                lambda v, a=is1, b=g2t, c=m2: v.tensor_scalar(a, b, c, None, ALU.is_ge),
                lambda v, a=s1t, b=is1: v.tensor_mul(a, b, iota[:]),
                lambda v, a=r1, b=s1t: v.reduce_max(a, b, axis=AX.X),
                lambda v, a=mk2, b=s1t, c=r1: v.tensor_scalar(a, b, c, None, ALU.is_equal),
            ]:
                bl.do("v", step, inc=0)
                bl.drain("v")
            c_v_mm = bl.do("v", lambda v, a=e2v, b=m2, c=m1: v.tensor_tensor(
                a, b, c, ALU.subtract), inc=1)
            c_s_e2 = bl.do("s", lambda s, a=e2v: s.activation(a, a, AF.Exp),
                           waits=[("v", c_v_mm)], inc=1)
            for step in [
                lambda v, a=p1, b=e2v: v.tensor_scalar(a, b, 1.0, None, ALU.add),
                lambda v, a=p1: v.reciprocal(a, a),
                lambda v, a=p2, b=e2v, c=p1: v.tensor_mul(a, b, c),
                lambda v, a=cb8, b=mk1, c=p1: v.tensor_scalar(a, b, c, None, ALU.mult),
                lambda v, a=cb8, b=mk2, c=p2: v.scalar_tensor_tensor(
                    a, b, c, a, ALU.mult, ALU.add),
            ]:
                bl.do("v", step, waits=[("s", c_s_e2)], inc=0)
                bl.drain("v")
            bl.do("v", lambda v: v.tensor_copy(
                af[0:1, O_T2 + 56:O_T2 + 57], af[0:1, O_T2 + 56:O_T2 + 57]), inc=1)
        c_v_comb = bl.cnt["v"]

        for e in range(NEXP):
            for hc in range(16):
                blkpar = hc % 2
                sl = ws_slot(blkpar)
                c_y1 = bl.do("y", lambda y, e=e, hc=hc, sl=sl: y.dma_start(
                    out=A(sl, 2048),
                    in_=w1s.ap()[e, hc].rearrange("p s dc c -> p (s dc c)")),
                    waits=ws_free[blkpar] + [("ys0", PREP_YS0), ("ys1", PREP_YS1)],
                    inc=("y%d" % blkpar, 16))
                for half in range(2):
                    k = hc * 2 + half
                    bank = ps[k % 2]
                    w = [("y%d" % blkpar, c_y1), ("v", c_v_comb), ("v", c_v_xln)] + bank_free[k % 2]
                    first = True
                    for dc in range(8):
                        lh = A(sl + dc * 128, 128)
                        ll = A(sl + 1024 + dc * 128, 128)
                        rh = A(O_XLN_H + dc * 1024 + half * 512, 512)
                        rl = A(O_XLN_L + dc * 1024 + half * 512, 512)
                        c_t = bl.do("t", lambda t, lh=lh, ll=ll, rh=rh, rl=rl,
                                    first=first, last=(dc == 7), bank=bank:
                                    mm3(t, bank[:], lh, ll, rh, rl, first, last),
                                    waits=(w if first else []), inc=1)
                        first = False
                    if half == 1:
                        ws_free[blkpar] = [("t", c_t)]
                    stg = Af(O_G32 + half * 512, 512)
                    bcol = b1s[:, e * 16 + hc:e * 16 + hc + 1]
                    c_s1 = bl.do("s", lambda s, bank=bank, stg=stg, bcol=bcol:
                                 s.activation(stg, bank[:], AF.Gelu, bias=bcol),
                                 waits=[("t", c_t), ("v", bl.cnt["v"])], inc=1)
                    bank_free[k % 2] = [("s", c_s1)]
                    o_h = O_H_H + hc * 1024 + half * 512
                    o_l = O_H_L + hc * 1024 + half * 512
                    bl.do("v", lambda v, stg=stg, o_h=o_h: v.tensor_copy(
                        A(o_h, 512), stg), waits=[("s", c_s1)], inc=0)
                    bl.drain("v")
                    bl.do("v", lambda v, stg=stg, o_h=o_h, o_l=o_l: v.tensor_tensor(
                        A(o_l, 512), stg, A(o_h, 512), ALU.subtract), inc=1)
            c_v_h = bl.cnt["v"]
            for th in range(2):
                grp_bank = {}
                for gi in range(8):
                    grp_bank[gi] = ps[gi]
                last_e = (e == NEXP - 1)
                for hc in range(16):
                    blkpar = hc % 2
                    sl = ws_slot(blkpar)
                    c_y2 = bl.do("y", lambda y, e=e, hc=hc, sl=sl: y.dma_start(
                        out=A(sl, 2048),
                        in_=w2s.ap()[e, hc].rearrange("p s f -> p (s f)")),
                        waits=ws_free[blkpar] + [("yw0", PREP_YW0), ("yw1", PREP_YW1)],
                        inc=("y%d" % blkpar, 16))
                    for gi in range(8):
                        tcl = th * 4 + gi // 2
                        dhalf = gi % 2
                        bank = grp_bank[gi]
                        lh = A(O_H_H + hc * 1024 + tcl * 128, 128)
                        ll = A(O_H_L + hc * 1024 + tcl * 128, 128)
                        rh = A(sl + dhalf * 512, 512)
                        rl = A(sl + 1024 + dhalf * 512, 512)
                        w = [("y%d" % blkpar, c_y2), ("v", c_v_h)]
                        if hc == 0:
                            w += bank_free[gi]
                        stop_here = last_e and hc == 15
                        c_t = bl.do("t", lambda t, lh=lh, ll=ll, rh=rh, rl=rl,
                                    hc=hc, bank=bank, stop_here=stop_here:
                                    mm3(t, bank[:], lh, ll, rh, rl, hc == 0,
                                        stop_here),
                                    waits=(w if (hc == 0 or gi == 0) else []), inc=1)
                    ws_free[blkpar] = [("t", c_t)]
                for gi in range(8):
                    tcl = th * 4 + gi // 2
                    dhalf = gi % 2
                    bank = grp_bank[gi]
                    if e < E_R:
                        c_t = bl.do("t", lambda t, bank=bank, e=e, dhalf=dhalf:
                                    sel_bias(t, bank[:], e, dhalf * 512, 512, True),
                                    inc=1)
                    elif e == E_R:
                        c_t = bl.do("t", lambda t, bank=bank, dhalf=dhalf: (
                            sel_bias(t, bank[:], 9, dhalf * 512, 512, False),
                            sel_bias(t, bank[:], 12, dhalf * 512, 512, True))[-1],
                            inc=1)
                    else:
                        c_t = bl.cnt["t"]
                    x1s = Af(O_X1 + tcl * 1024 + dhalf * 512, 512)
                    if e < E_R:
                        wcol = Af(O_COMB + tcl * 8 + e, 1)
                        c_v = bl.do("v", lambda v, bank=bank, x1s=x1s, wcol=wcol:
                                    v.scalar_tensor_tensor(x1s, bank[:], wcol, x1s,
                                                           ALU.mult, ALU.add),
                                    waits=[("t", c_t)], inc=1)
                    else:
                        c_v = bl.do("v", lambda v, bank=bank, x1s=x1s:
                                    v.scalar_tensor_tensor(x1s, bank[:], 0.5, x1s,
                                                           ALU.mult, ALU.add),
                                    waits=[("t", c_t)], inc=1)
                    bank_free[gi] = [("v", c_v)]

        c_v_done = bl.cnt["v"]
        if hf == 0:
            last_store_g = bl.do("g", lambda g, sq=sq: g.dma_start(
                out=y_out[sq, 32:1056, :].rearrange("(c p) f -> p c f", p=128),
                in_=Af(O_X1, 8192).rearrange("p (c f) -> p c f", f=1024)),
                waits=[("v", c_v_done)], inc=("go", 16))
        else:
            bl.do("g", lambda g, sq=sq: g.dma_start(
                out=y_out[sq, 1056:1952, :].rearrange("(c p) f -> p c f", p=128),
                in_=Af(O_X1, 7168).rearrange("p (c f) -> p c f", f=1024)),
                waits=[("v", c_v_done)], inc=("go", 16))
            bl.do("g", lambda g, sq=sq: g.dma_start(
                out=y_out[sq, 1952:2048, :],
                in_=af[0:96, O_X1 + 7168:O_X1 + 8192]), inc=("go", 16))
            last_store_g = bl.do("g", lambda g, sq=sq: g.dma_start(
                out=y_out[sq, 0:32, :],
                in_=af[96:128, O_X1 + 7168:O_X1 + 8192]), inc=("go", 16))
        x1_free = [("v", c_v_done), ("t", bl.cnt["t"])]

    bl.do("g", lambda g: g.nop(),
          waits=[("gc", bl.cnt["gc"]), ("gp0", bl.cnt["gp0"]),
                 ("gp1", bl.cnt["gp1"]), ("gp2", bl.cnt["gp2"]),
                 ("gs0", bl.cnt["gs0"]),
                 ("gs1", bl.cnt["gs1"]), ("gx", bl.cnt["gx"]),
                 ("go", bl.cnt["go"])], inc=0)
    bl.do("y", lambda y: y.nop(),
          waits=[("y0", bl.cnt["y0"]), ("y1", bl.cnt["y1"]), ("yo", bl.cnt["yo"]),
                 ("ys0", bl.cnt["ys0"]), ("ys1", bl.cnt["ys1"]),
                 ("yw0", bl.cnt["yw0"]), ("yw1", bl.cnt["yw1"])], inc=0)

    with nc.Block() as block:
        def run(e):
            def body(eng):
                for (waits, fn, inc) in bl.ops[e]:
                    for (se, pv) in waits:
                        eng.wait_ge(sem[se], pv)
                    ins = fn(eng)
                    if inc is not None:
                        ins.then_inc(sem[inc[0]], inc[1])
            return body

        block.tensor(run("t"))
        block.vector(run("v"))
        block.scalar(run("s"))
        block.sync(run("y"))
        block.gpsimd(run("g"))

    es.close()
    return nc


def _consts():
    identv = np.eye(128, dtype=np.float32)
    selv = np.zeros((16, 16 * 128), dtype=ml_dtypes.bfloat16)
    for r in range(16):
        selv[r, r * 128:(r + 1) * 128] = 1.0
    iotav = np.tile((float(E_R) - np.arange(E_R)).astype(np.float32), (128, 1))
    return identv, selv, iotav


_NC_CACHE = None


def kernel(L0_attn_x, L0_attn_in_proj_w, L0_attn_in_proj_b, L0_attn_out_w,
           L0_attn_out_b, L0_ffn_router_w, L0_ffn_router_b, L0_ffn_w1, L0_ffn_b1,
           L0_ffn_w2, L0_ffn_b2, L0_ffn_sw1, L0_ffn_sb1, L0_ffn_sw2, L0_ffn_sb2):
    global _NC_CACHE
    if _NC_CACHE is None:
        _NC_CACHE = build_nc()
    nc = _NC_CACHE
    identv, selv, iotav = _consts()
    x = np.ascontiguousarray(np.asarray(L0_attn_x, dtype=np.float32))
    common = dict(
        in_proj_w=np.ascontiguousarray(L0_attn_in_proj_w, dtype=np.float32),
        in_proj_b=np.ascontiguousarray(L0_attn_in_proj_b, dtype=np.float32),
        out_w=np.ascontiguousarray(L0_attn_out_w, dtype=np.float32),
        out_b=np.ascontiguousarray(L0_attn_out_b, dtype=np.float32),
        router_w=np.ascontiguousarray(L0_ffn_router_w, dtype=np.float32),
        router_b=np.ascontiguousarray(L0_ffn_router_b, dtype=np.float32),
        w1=np.ascontiguousarray(L0_ffn_w1, dtype=np.float32),
        b1=np.ascontiguousarray(L0_ffn_b1, dtype=np.float32),
        w2=np.ascontiguousarray(L0_ffn_w2, dtype=np.float32),
        b2=np.ascontiguousarray(L0_ffn_b2, dtype=np.float32),
        sw1=np.ascontiguousarray(L0_ffn_sw1, dtype=np.float32),
        sb1=np.ascontiguousarray(L0_ffn_sb1, dtype=np.float32),
        sw2=np.ascontiguousarray(L0_ffn_sw2, dtype=np.float32),
        sb2=np.ascontiguousarray(L0_ffn_sb2, dtype=np.float32),
        identc=identv, selc=selv, iotac=iotav,
    )
    in_maps = []
    for c in range(N_CORES):
        m = dict(common)
        m["x"] = np.ascontiguousarray(x[c * SEQ_PER_CORE:(c + 1) * SEQ_PER_CORE])
        in_maps.append(m)
    res = run_bass_kernel_spmd(nc, in_maps, core_ids=list(range(N_CORES)))
    return np.concatenate([r["yo"] for r in res.results], axis=0)


# version 08
# speedup vs baseline: 1.1382x; 1.1382x over previous
import contextlib
import numpy as np
import ml_dtypes

import concourse.bass as bass
import concourse.mybir as mybir
from concourse.bass_utils import run_bass_kernel_spmd

F32 = mybir.dt.float32
BF16 = mybir.dt.bfloat16
AF = mybir.ActivationFunctionType
ALU = mybir.AluOpType
AX = mybir.AxisListType

B_FULL, L, D = 16, 2048, 1024
N_CORES = 8
SEQ_PER_CORE = B_FULL // N_CORES
H = 8
HD = 128
E_R, E_S, NEXP = 8, 2, 10
HID = 2048
INV_SQRT_HD = 1.0 / float(np.sqrt(HD))
EPS = 1e-5

ENGINES = ("t", "v", "s", "y", "g")


SEMS = ("t", "v", "s", "gc", "gp0", "gp1", "gp2", "gs0", "gs1",
        "y0", "y1", "yo", "ys0", "ys1", "yw0", "yw1",
        "xc0", "xc1", "xc2", "xc3", "xc4", "xc5", "xc6", "xc7")


class Builder:

    def __init__(self):
        self.ops = {e: [] for e in ENGINES}
        self.cnt = {s: 0 for s in SEMS}
        self.waited = {e: {s: 0 for s in SEMS} for e in ENGINES}

    def do(self, e, fn, waits=(), inc=None):
        w = []
        for (se, pv) in waits:
            if pv is None or pv <= 0:
                continue
            assert pv <= self.cnt[se], (
                f"wait on future count: {e} waits {se}>={pv}, have {self.cnt[se]}")
            if pv > self.waited[e][se]:
                self.waited[e][se] = pv
                w.append((se, pv))
        if isinstance(inc, int):
            inc = (e, inc) if inc else None
        self.ops[e].append((w, fn, inc))
        if inc is not None:
            self.cnt[inc[0]] += inc[1]
            return self.cnt[inc[0]]
        return None

    def drain(self, e):
        self.ops[e].append(([], lambda eng: eng.drain(), None))


def build_nc(nranges=4):
    nc = bass.Bass("TRN2", target_bir_lowering=False)
    bl = Builder()

    x_in = nc.dram_tensor("x", [SEQ_PER_CORE, L, D], F32, kind="ExternalInput").ap()
    ipw = nc.dram_tensor("in_proj_w", [3 * D, D], F32, kind="ExternalInput").ap()
    ipb = nc.dram_tensor("in_proj_b", [3 * D], F32, kind="ExternalInput").ap()
    outw = nc.dram_tensor("out_w", [D, D], F32, kind="ExternalInput").ap()
    outb = nc.dram_tensor("out_b", [D], F32, kind="ExternalInput").ap()
    rw = nc.dram_tensor("router_w", [D, E_R], F32, kind="ExternalInput").ap()
    rb = nc.dram_tensor("router_b", [E_R], F32, kind="ExternalInput").ap()
    w1 = nc.dram_tensor("w1", [E_R, D, HID], F32, kind="ExternalInput").ap()
    b1 = nc.dram_tensor("b1", [E_R, HID], F32, kind="ExternalInput").ap()
    w2 = nc.dram_tensor("w2", [E_R, HID, D], F32, kind="ExternalInput").ap()
    b2 = nc.dram_tensor("b2", [E_R, D], F32, kind="ExternalInput").ap()
    sw1 = nc.dram_tensor("sw1", [E_S, D, HID], F32, kind="ExternalInput").ap()
    sb1 = nc.dram_tensor("sb1", [E_S, HID], F32, kind="ExternalInput").ap()
    sw2 = nc.dram_tensor("sw2", [E_S, HID, D], F32, kind="ExternalInput").ap()
    sb2 = nc.dram_tensor("sb2", [E_S, D], F32, kind="ExternalInput").ap()
    identc = nc.dram_tensor("identc", [128, 128], F32, kind="ExternalInput").ap()
    selc = nc.dram_tensor("selc", [16, 16 * 128], BF16, kind="ExternalInput").ap()
    iotac = nc.dram_tensor("iotac", [128, E_R], F32, kind="ExternalInput").ap()
    y_out = nc.dram_tensor("yo", [SEQ_PER_CORE, L, D], F32, kind="ExternalOutput").ap()

    qkvT = nc.dram_tensor("qkvT", [24, 128, 2, 8, 128], BF16)
    woT = nc.dram_tensor("woT", [8, 128, 2, D], BF16)
    w1s = nc.dram_tensor("w1s", [NEXP, 16, 128, 2, 8, 128], BF16)
    w2s = nc.dram_tensor("w2s", [NEXP, 16, 128, 2, D], BF16)

    NB = 66560
    NF = 16384
    ab = nc.alloc_sbuf_tensor("ab", [128, NB], BF16)
    af = nc.alloc_sbuf_tensor("af", [128, NF], F32)

    O_XLN_H, O_XLN_L = 0, 8192
    O_Q_H, O_Q_L = 16384, 20480
    O_K_H, O_K_L = 24576, 28672
    O_OT_H = 16384
    O_V_H, O_V_L = 32768, 40960
    O_O_H, O_O_L = 49152, 53248
    O_H_H, O_H_L = 16384, 32768
    O_WS = 57344
    O_AT_H, O_AT_L = 65536, 66048

    O_X1 = 0
    O_V32 = 8192
    O_QK32 = 12288
    O_SC32 = 13312
    O_G32 = 8192
    O_STAT = 14864
    O_GATE = 14912
    O_COMB = 14976
    O_T2 = 15040
    O_STG = 15104
    O_JUNK = 15104
    LD_A, LD_B = 4096, 2048

    ident = nc.alloc_sbuf_tensor("ident", [128, 128], F32)
    sel = nc.alloc_sbuf_tensor("sel", [16, 16 * 128], BF16)
    iota = nc.alloc_sbuf_tensor("iota", [128, E_R], F32)
    inb = nc.alloc_sbuf_tensor("inb", [128, 24], F32)
    b1s = nc.alloc_sbuf_tensor("b1s", [128, NEXP * 16], F32)
    brow = nc.alloc_sbuf_tensor("brow", [16, 2048], BF16)
    rwt = nc.alloc_sbuf_tensor("rwt", [128, 8, 2, E_R], BF16)
    stat = nc.alloc_sbuf_tensor("stat", [128, 48], F32)
    ps = [nc.alloc_psum_tensor(f"ps{i}", [128, 512], F32) for i in range(8)]

    es = contextlib.ExitStack()
    sem = {s: es.enter_context(nc.semaphore(f"sem_{s}")) for s in SEMS}

    bank_free = {i: [] for i in range(8)}

    def A(off, n):
        return ab[:, off:off + n]

    def Af(off, n):
        return af[:, off:off + n]

    def mm3(t, out_ap, lh, ll, rh, rl, start, stop):
        t.matmul(out_ap, lh, rh, start=start, stop=False)
        t.matmul(out_ap, lh, rl, start=False, stop=False)
        return t.matmul(out_ap, ll, rh, start=False, stop=stop)

    def sel_bias(t, out_ap, row, col0, n, stop):
        t.matmul(out_ap, sel[:, row * 128:(row + 1) * 128],
                 brow[:, col0:col0 + n], start=False, stop=False)
        return t.matmul(out_ap, sel[:, row * 128:(row + 1) * 128],
                        brow[:, 1024 + col0:1024 + col0 + n], start=False, stop=stop)

    bl.do("g", lambda g: g.dma_start(out=ident[:], in_=identc[:, :]), inc=("gc", 16))
    bl.do("g", lambda g: g.dma_start(out=sel[:], in_=selc[:, :]), inc=("gc", 16))
    bl.do("g", lambda g: g.dma_start(out=iota[:], in_=iotac[:, :]), inc=("gc", 16))
    bl.do("g", lambda g: g.dma_start(
        out=inb[:], in_=ipb.rearrange("(f p) -> p f", p=128),
        allow_slow_non_contiguous=True), inc=("gc", 16))
    for e in range(E_R):
        bl.do("g", lambda g, e=e: g.dma_start(
            out=b1s[:, e * 16:(e + 1) * 16],
            in_=b1[e].rearrange("(h p) -> p h", p=128),
            allow_slow_non_contiguous=True), inc=("gc", 16))
    for e in range(E_S):
        bl.do("g", lambda g, e=e: g.dma_start(
            out=b1s[:, (E_R + e) * 16:(E_R + e + 1) * 16],
            in_=sb1[e].rearrange("(h p) -> p h", p=128),
            allow_slow_non_contiguous=True), inc=("gc", 16))
    bl.do("v", lambda v: v.memset(af[0:1, O_T2 + 56:O_T2 + 57], 0.0), inc=0)
    c_vz = bl.do("v", lambda v: v.memset(af[0:16, 0:1024], 0.0), inc=1)
    bl.do("g", lambda g: g.dma_start(out=af[0:8, 0:1024], in_=b2[:, :]),
          waits=[("v", c_vz)], inc=("gc", 16))
    bl.do("g", lambda g: g.dma_start(
        out=af[8:9, 0:1024], in_=outb.rearrange("(o n) -> o n", o=1)), inc=("gc", 16))
    bl.do("g", lambda g: g.dma_start(
        out=af[9:10, 0:1024], in_=sb2[0:1, :]), inc=("gc", 16))
    bl.do("g", lambda g: g.dma_start(
        out=af[12:13, 0:1024], in_=sb2[1:2, :]), inc=("gc", 16))
    bl.do("g", lambda g: g.dma_start(
        out=af[10:11, 0:E_R], in_=rb.rearrange("(o n) -> o n", o=1)), inc=("gc", 16))
    GC = bl.do("g", lambda g: g.dma_start(
        out=af[:, 6144:6208].rearrange("p (dc e) -> p dc e", dc=8),
        in_=rw.rearrange("(dc p) e -> p dc e", p=128)), inc=("gc", 16))
    bl.do("v", lambda v: v.tensor_copy(brow[0:16, 0:1024], af[0:16, 0:1024]),
          waits=[("gc", GC)], inc=0)
    bl.drain("v")
    bl.do("v", lambda v: v.tensor_tensor(
        brow[0:16, 1024:2048], af[0:16, 0:1024], brow[0:16, 0:1024], ALU.subtract),
        inc=1)
    bl.do("v", lambda v: v.tensor_copy(
        rwt[:, :, 0, :], af[:, 6144:6208].rearrange("p (dc e) -> p dc e", dc=8)),
        inc=0)
    bl.drain("v")
    bl.do("v", lambda v: v.tensor_tensor(
        rwt[:, :, 1, :], af[:, 6144:6208].rearrange("p (dc e) -> p dc e", dc=8),
        rwt[:, :, 0, :], ALU.subtract), inc=1)

    ld_par = [0]
    ld_cons = {0: [], 1: [], 2: []}
    LD_OFF = {0: 4096, 1: 2048, 2: 6144}

    def prep_load(in_ap_fn, n=1024):
        par = ld_par[0]
        ld_par[0] = (par + 1) % 3
        off = LD_OFF[par]
        c = bl.do("g", lambda g, off=off, n=n: g.dma_start(
            out=af[:, off:off + n], in_=in_ap_fn()),
            waits=ld_cons[par], inc=("gp%d" % par, 16))
        return par, off, c

    qk_store_cnt = [0, 0]
    for fc in range(24):
        par, off, c_ld = prep_load(lambda fc=fc: ipw[fc * 128:(fc + 1) * 128, :])
        sp = fc % 2
        stg0 = sp * 2048
        for dc in range(8):
            bank = ps[dc % 4]
            w = [("gp%d" % par, c_ld), ("gs%d" % sp, qk_store_cnt[sp])] + bank_free[dc % 4]
            if fc == 0 and dc == 0:
                w += [("gc", GC)]
            c_t = bl.do("t", lambda t, dc=dc, bank=bank, off=off: t.transpose(
                bank[:, 0:128], af[:, off + dc * 128:off + (dc + 1) * 128],
                ident[:]), waits=w, inc=1)
            c_s = bl.do("s", lambda s, dc=dc, bank=bank, stg0=stg0: s.activation(
                ab[:, stg0 + dc * 128:stg0 + (dc + 1) * 128], bank[:, 0:128],
                AF.Copy), waits=[("t", c_t)], inc=1)
            c_v = bl.do("v", lambda v, dc=dc, bank=bank, stg0=stg0: v.tensor_tensor(
                ab[:, stg0 + 1024 + dc * 128:stg0 + 1024 + (dc + 1) * 128],
                bank[:, 0:128], ab[:, stg0 + dc * 128:stg0 + (dc + 1) * 128],
                ALU.subtract), waits=[("s", c_s)], inc=1)
            bank_free[dc % 4] = [("v", c_v)]
        ld_cons[par] = [("t", bl.cnt["t"])]
        qk_store_cnt[sp] = bl.do("g", lambda g, fc=fc, stg0=stg0: g.dma_start(
            out=qkvT.ap()[fc].rearrange("p s dc c -> p (s dc c)"),
            in_=ab[:, stg0:stg0 + 2048]), waits=[("v", bl.cnt["v"])],
            inc=("gs%d" % sp, 16))

    for fo in range(8):
        par, off, c_ld = prep_load(lambda fo=fo: outw[fo * 128:(fo + 1) * 128, :])
        for fi in range(8):
            bank = ps[fi % 4]
            w = [("gp%d" % par, c_ld)] + bank_free[fi % 4]
            if fo == 0 and fi < 2:
                w += [("gs0", qk_store_cnt[0]), ("gs1", qk_store_cnt[1])]
            c_t = bl.do("t", lambda t, fi=fi, bank=bank, off=off: t.transpose(
                bank[:, 0:128], af[:, off + fi * 128:off + (fi + 1) * 128],
                ident[:]), waits=w, inc=1)
            o_h = 4096 + fi * 2048 + fo * 128
            o_l = 4096 + fi * 2048 + 1024 + fo * 128
            c_s = bl.do("s", lambda s, bank=bank, o_h=o_h: s.activation(
                ab[:, o_h:o_h + 128], bank[:, 0:128], AF.Copy),
                waits=[("t", c_t)], inc=1)
            c_v = bl.do("v", lambda v, bank=bank, o_h=o_h, o_l=o_l: v.tensor_tensor(
                ab[:, o_l:o_l + 128], bank[:, 0:128], ab[:, o_h:o_h + 128],
                ALU.subtract), waits=[("s", c_s)], inc=1)
            bank_free[fi % 4] = [("v", c_v)]
        ld_cons[par] = [("t", bl.cnt["t"])]
    c_v = bl.cnt["v"]
    wo_store_cnt = qk_store_cnt[0]
    for fi in range(8):
        wo_store_cnt = bl.do("g", lambda g, fi=fi: g.dma_start(
            out=woT.ap()[fi].rearrange("p s f -> p (s f)"),
            in_=ab[:, 4096 + fi * 2048:4096 + (fi + 1) * 2048]),
            waits=[("v", c_v)], inc=("gs0", 16))
    GS0, GS1 = bl.cnt["gs0"], bl.cnt["gs1"]

    ys_cnt = [0, 0]
    yw_cnt = [0, 0]
    for e in range(NEXP):
        src1 = w1 if e < E_R else sw1
        src2 = w2 if e < E_R else sw2
        ei = e if e < E_R else e - E_R
        ep = e % 2
        hi_stg = ab[:, 0:16384].rearrange("p (hc dc c) -> p hc dc c", dc=8, c=128)
        lo_stg = ab[:, 16384:32768].rearrange("p (hc dc c) -> p hc dc c", dc=8, c=128)
        for dc in range(8):
            par, off, c_ld = prep_load(
                lambda dc=dc, src1=src1, ei=ei: src1[ei, dc * 128:(dc + 1) * 128, :],
                n=2048)
            w = [("gp%d" % par, c_ld), ("ys%d" % (1 - ep), ys_cnt[1 - ep])]
            if e == 0 and dc == 0:
                w += [("gs0", GS0), ("gs1", GS1)]
            in_v = af[:, off:off + 2048].rearrange("p (hc c) -> p hc c", c=128)
            c_s = bl.do("s", lambda s, dc=dc, in_v=in_v: s.activation(
                hi_stg[:, :, dc, :], in_v, AF.Copy), waits=w, inc=1)
            c_v = bl.do("v", lambda v, dc=dc, in_v=in_v: v.tensor_tensor(
                lo_stg[:, :, dc, :], in_v, hi_stg[:, :, dc, :], ALU.subtract),
                waits=[("s", c_s)], inc=1)
            ld_cons[par] = [("v", c_v)]
        c_v = bl.cnt["v"]
        for hc in range(16):
            bl.do("y", lambda y, e=e, hc=hc: y.dma_start(
                out=w1s.ap()[e, hc, :, 0].rearrange("p dc c -> p (dc c)"),
                in_=ab[:, hc * 1024:(hc + 1) * 1024]),
                waits=[("v", c_v)], inc=("ys%d" % ep, 16))
            ys_cnt[ep] = bl.do("y", lambda y, e=e, hc=hc: y.dma_start(
                out=w1s.ap()[e, hc, :, 1].rearrange("p dc c -> p (dc c)"),
                in_=ab[:, 16384 + hc * 1024:16384 + (hc + 1) * 1024]),
                inc=("ys%d" % ep, 16))
        for hc in range(16):
            par, off, c_ld = prep_load(
                lambda hc=hc, src2=src2, ei=ei: src2[ei, hc * 128:(hc + 1) * 128, :])
            wp = hc % 2
            wst = 32768 + wp * 2048
            c_s = bl.do("s", lambda s, off=off, wst=wst: s.activation(
                ab[:, wst:wst + 1024], af[:, off:off + 1024], AF.Copy),
                waits=[("gp%d" % par, c_ld), ("yw%d" % wp, yw_cnt[wp])], inc=1)
            c_v2 = bl.do("v", lambda v, off=off, wst=wst: v.tensor_tensor(
                ab[:, wst + 1024:wst + 2048], af[:, off:off + 1024],
                ab[:, wst:wst + 1024], ALU.subtract), waits=[("s", c_s)], inc=1)
            ld_cons[par] = [("v", c_v2)]
            yw_cnt[wp] = bl.do("y", lambda y, e=e, hc=hc, wst=wst: y.dma_start(
                out=w2s.ap()[e, hc].rearrange("p s f -> p (s f)"),
                in_=ab[:, wst:wst + 2048]), waits=[("v", c_v2)],
                inc=("yw%d" % wp, 16))

    PREP_YS0, PREP_YS1 = bl.cnt["ys0"], bl.cnt["ys1"]
    PREP_YW0, PREP_YW1 = bl.cnt["yw0"], bl.cnt["yw1"]
    bank_free = {i: [("v", bl.cnt["v"]), ("s", bl.cnt["s"])] for i in range(8)}

    ws_free = [[], []]
    last_store_g = 0
    x1_free = [("v", bl.cnt["v"]), ("ys0", PREP_YS0), ("ys1", PREP_YS1),
               ("yw0", PREP_YW0), ("yw1", PREP_YW1), ("gs0", GS0), ("gs1", GS1)]

    def ws_slot(par):
        return O_WS + par * 4096

    def ln_stats(tc0, ntc, x_ready_w):
        for i in range(ntc):
            tc = tc0 + i
            x = Af(O_X1 + tc * 1024, 1024)
            w = x_ready_w(tc) if callable(x_ready_w) else x_ready_w
            bl.do("v", lambda v, x=x, tc=tc: v.reduce_sum(
                stat[:, tc:tc + 1], x, axis=AX.X), waits=w, inc=0)
            bl.do("v", lambda v, x=x, tc=tc: v.scalar_tensor_tensor(
                Af(O_JUNK, 1024), x, 1.0, x, ALU.mult, ALU.mult,
                accum_out=stat[:, 8 + tc:8 + tc + 1]), inc=0)
            bl.drain("v")
        a, b = tc0, tc0 + ntc
        bl.do("v", lambda v: v.tensor_scalar(
            stat[:, 16 + a:16 + b], stat[:, a:b], 1.0 / 1024.0, None, ALU.mult), inc=0)
        bl.do("v", lambda v: v.tensor_scalar(
            stat[:, 24 + a:24 + b], stat[:, 8 + a:8 + b], 1.0 / 1024.0, None, ALU.mult), inc=0)
        bl.drain("v")
        bl.do("v", lambda v: v.tensor_mul(
            stat[:, 40 + a:40 + b], stat[:, 16 + a:16 + b], stat[:, 16 + a:16 + b]), inc=0)
        bl.drain("v")
        bl.do("v", lambda v: v.tensor_tensor(
            stat[:, 24 + a:24 + b], stat[:, 24 + a:24 + b], stat[:, 40 + a:40 + b],
            ALU.subtract), inc=0)
        bl.drain("v")
        c_v = bl.do("v", lambda v: v.tensor_scalar(
            stat[:, 24 + a:24 + b], stat[:, 24 + a:24 + b], EPS, None, ALU.add), inc=1)
        bl.do("s", lambda s: s.activation(
            stat[:, 40 + a:40 + b], stat[:, 24 + a:24 + b], AF.Ln),
            waits=[("v", c_v)], inc=1)
        c_v = bl.do("v", lambda v: v.tensor_scalar(
            stat[:, 40 + a:40 + b], stat[:, 40 + a:40 + b], -0.5, None, ALU.mult),
            waits=[("s", bl.cnt["s"])], inc=1)
        return bl.do("s", lambda s: s.activation(
            stat[:, 32 + a:32 + b], stat[:, 40 + a:40 + b], AF.Exp),
            waits=[("v", c_v)], inc=1)

    def ln_apply(tc0, ntc, c_s_rs):
        ln_last_t = {0: 0, 1: 0}
        for i in range(ntc):
            tc = tc0 + i
            for fg in range(2):
                it = i * 2 + fg
                par = it % 2
                stg = Af(O_STG + par * 512, 512)
                bank = ps[par]
                c_v0 = bl.do("v", lambda v, tc=tc, fg=fg, stg=stg: v.tensor_scalar(
                    stg, Af(O_X1 + tc * 1024 + fg * 512, 512),
                    stat[:, 16 + tc:16 + tc + 1], stat[:, 32 + tc:32 + tc + 1],
                    ALU.subtract, ALU.mult),
                    waits=[("s", c_s_rs), ("t", ln_last_t[par])], inc=1)
                for j in range(4):
                    c_t = bl.do("t", lambda t, stg=stg, bank=bank, j=j: t.transpose(
                        bank[:, j * 128:(j + 1) * 128],
                        stg[:, j * 128:(j + 1) * 128], ident[:]),
                        waits=([("v", c_v0)] + bank_free[par]) if j == 0 else [],
                        inc=1)
                ln_last_t[par] = bl.cnt["t"]
                col = tc * 128
                fc0 = fg * 4
                hi_view = A(O_XLN_H, 8192).rearrange(
                    "p (f c) -> p f c", c=1024)[:, fc0:fc0 + 4, col:col + 128]
                lo_view = A(O_XLN_L, 8192).rearrange(
                    "p (f c) -> p f c", c=1024)[:, fc0:fc0 + 4, col:col + 128]
                bank_view = bank[:, 0:512].rearrange("p (f c) -> p f c", c=128)
                c_s = bl.do("s", lambda s, bank_view=bank_view, hi_view=hi_view:
                            s.activation(hi_view, bank_view, AF.Copy),
                            waits=[("t", c_t)], inc=1)
                c_v = bl.do("v", lambda v, bank_view=bank_view, hi_view=hi_view,
                            lo_view=lo_view: v.tensor_tensor(
                    lo_view, bank_view, hi_view, ALU.subtract),
                    waits=[("s", c_s)], inc=1)
                bank_free[par] = [("v", c_v)]

    def evac_split(bank, bias_col, scale, o_h, o_l, c_t, par, n=512):
        stg = Af(O_QK32 + par * 512, n)
        if scale is not None:
            c_v0 = bl.do("v", lambda v: v.tensor_scalar(
                stg, bank[:, 0:n], bias_col, scale, ALU.add, ALU.mult),
                waits=[("t", c_t)], inc=1)
        else:
            c_v0 = bl.do("v", lambda v: v.tensor_scalar(
                stg, bank[:, 0:n], bias_col, None, ALU.add),
                waits=[("t", c_t)], inc=1)
        c_s = bl.do("s", lambda s: s.activation(A(o_h, n), stg, AF.Copy),
                    waits=[("v", c_v0)], inc=1)
        c_v = bl.do("v", lambda v: v.tensor_tensor(
            A(o_l, n), stg, A(o_h, n), ALU.subtract), waits=[("s", c_s)], inc=1)
        return c_v, c_v0

    x_ready = {}

    def chunk_store(sq, hf, tc, waits):
        t0_ = 32 + hf * 1024 + tc * 128
        if t0_ + 128 <= L:
            return bl.do("g", lambda g, sq=sq, t0_=t0_, tc=tc: g.dma_start(
                out=y_out[sq, t0_:t0_ + 128, :],
                in_=Af(O_X1 + tc * 1024, 1024)), waits=waits, inc=("xc%d" % tc, 16))
        bl.do("g", lambda g, sq=sq, t0_=t0_, tc=tc: g.dma_start(
            out=y_out[sq, t0_:L, :], in_=af[0:L - t0_, O_X1 + tc * 1024:O_X1 + (tc + 1) * 1024]),
            waits=waits, inc=("xc%d" % tc, 16))
        return bl.do("g", lambda g, sq=sq, t0_=t0_, tc=tc: g.dma_start(
            out=y_out[sq, 0:t0_ + 128 - L, :],
            in_=af[L - t0_:128, O_X1 + tc * 1024:O_X1 + (tc + 1) * 1024]),
            inc=("xc%d" % tc, 16))

    def chunk_load(sq, hf, tc, waits):
        t0_ = 32 + hf * 1024 + tc * 128
        if t0_ + 128 <= L:
            c = bl.do("g", lambda g, sq=sq, t0_=t0_, tc=tc: g.dma_start(
                out=Af(O_X1 + tc * 1024, 1024),
                in_=x_in[sq, t0_:t0_ + 128, :]), waits=waits, inc=("xc%d" % tc, 16))
        else:
            bl.do("g", lambda g, sq=sq, t0_=t0_, tc=tc: g.dma_start(
                out=af[0:L - t0_, O_X1 + tc * 1024:O_X1 + (tc + 1) * 1024],
                in_=x_in[sq, t0_:L, :]), waits=waits, inc=("xc%d" % tc, 16))
            c = bl.do("g", lambda g, sq=sq, t0_=t0_, tc=tc: g.dma_start(
                out=af[L - t0_:128, O_X1 + tc * 1024:O_X1 + (tc + 1) * 1024],
                in_=x_in[sq, 0:t0_ + 128 - L, :]), inc=("xc%d" % tc, 16))
        x_ready[tc] = ("xc%d" % tc, c)

    ranges = [(s, h) for s in range(SEQ_PER_CORE) for h in range(2)][:nranges]
    for ri, (sq, hf) in enumerate(ranges):
        if ri == 0:
            for tc in range(8):
                chunk_load(sq, hf, tc, waits=list(x1_free))

        ln_xln = {}
        c_rs = ln_stats(0, 4, lambda tc: [x_ready[tc]])
        ln_apply(0, 4, c_rs)
        ln_xln[0] = bl.cnt["v"]
        for sub in range(2):
            t0 = sub * 512
            tc0 = sub * 4
            c_v_xln = ln_xln[sub]

            for fc in range(24):
                par = fc % 2
                bpar = fc % 4
                c_y = bl.do("y", lambda y, fc=fc, par=par: y.dma_start(
                    out=A(ws_slot(par), 2048),
                    in_=qkvT.ap()[fc].rearrange("p s dc c -> p (s dc c)")),
                    waits=ws_free[par] + [("gs0", GS0), ("gs1", GS1)],
                    inc=("y%d" % par, 16))
                bank = ps[2 + bpar]
                w = [("y%d" % par, c_y), ("v", c_v_xln)] + bank_free[2 + bpar]
                first = True
                for dc in range(8):
                    sl = ws_slot(par)
                    lh = A(sl + dc * 128, 128)
                    ll = A(sl + 1024 + dc * 128, 128)
                    rh = A(O_XLN_H + dc * 1024 + t0, 512)
                    rl = A(O_XLN_L + dc * 1024 + t0, 512)
                    c_t = bl.do("t", lambda t, lh=lh, ll=ll, rh=rh, rl=rl,
                                first=first, last=(dc == 7), bank=bank:
                                mm3(t, bank[:], lh, ll, rh, rl, first, last),
                                waits=(w if first else []), inc=1)
                    first = False
                ws_free[par] = [("t", c_t)]
                if fc < 8:
                    c_v, c_v0 = evac_split(bank, inb[:, fc:fc + 1], INV_SQRT_HD,
                                           O_Q_H + fc * 512, O_Q_L + fc * 512, c_t, par)
                    bank_free[2 + bpar] = [("v", c_v0)]
                elif fc < 16:
                    f = fc - 8
                    c_v, c_v0 = evac_split(bank, inb[:, fc:fc + 1], None,
                                           O_K_H + f * 512, O_K_L + f * 512, c_t, par)
                    bank_free[2 + bpar] = [("v", c_v0)]
                else:
                    f = fc - 16
                    c_v0 = bl.do("v", lambda v, bank=bank, fc=fc, f=f: v.tensor_scalar(
                        Af(O_V32 + f * 512, 512), bank[:], inb[:, fc:fc + 1], None,
                        ALU.add), waits=[("t", c_t)], inc=1)
                    bank_free[2 + bpar] = [("v", c_v0)]
            c_v_v32 = bl.cnt["v"]

            for wi in range(8):
                for h in range(8):
                    k = wi * 8 + h
                    bank = ps[4 + k % 2]
                    c_t = bl.do("t", lambda t, wi=wi, h=h, bank=bank: t.transpose(
                        bank[0:64, 0:128],
                        Af(O_V32 + h * 512 + wi * 64, 64), ident[:]),
                        waits=[("v", c_v_v32)] + bank_free[4 + k % 2], inc=1)
                    o_h = O_V_H + wi * 1024 + h * 128
                    o_l = O_V_L + wi * 1024 + h * 128
                    c_s = bl.do("s", lambda s, bank=bank, o_h=o_h: s.activation(
                        ab[0:64, o_h:o_h + 128], bank[0:64, 0:128], AF.Copy),
                        waits=[("t", c_t)], inc=1)
                    c_v = bl.do("v", lambda v, bank=bank, o_h=o_h, o_l=o_l:
                                v.tensor_tensor(ab[0:64, o_l:o_l + 128],
                                                bank[0:64, 0:128],
                                                ab[0:64, o_h:o_h + 128], ALU.subtract),
                                waits=[("s", c_s)], inc=1)
                    bank_free[4 + k % 2] = [("v", c_v)]
            c_v_vsplit = bl.cnt["v"]

            s32 = af[0:64, O_SC32:O_SC32 + 512]
            e32 = af[0:64, O_SC32 + 512:O_SC32 + 1024]
            a32p = [af[0:64, O_SC32 + 1024:O_SC32 + 1536],
                    af[0:64, O_QK32:O_QK32 + 512]]
            mx = af[0:64, O_SC32 + 1536:O_SC32 + 1544]
            sm = af[0:64, O_SC32 + 1544:O_SC32 + 1552]
            win_state = {}

            def emit_scores(wi):
                bankS = ps[6 + wi % 2]
                w0 = [("v", c_v_vsplit)] + bank_free[6 + wi % 2]
                for h in range(8):
                    lh = A(O_Q_H + h * 512 + wi * 64, 64)
                    ll = A(O_Q_L + h * 512 + wi * 64, 64)
                    rh = A(O_K_H + h * 512 + wi * 64, 64)
                    rl = A(O_K_L + h * 512 + wi * 64, 64)
                    out = bankS[0:64, h * 64:(h + 1) * 64]
                    bl.do("t", lambda t, lh=lh, ll=ll, rh=rh, rl=rl, out=out: (
                        t.matmul(out, lh, rh, start=True, stop=False),
                        t.matmul(out, lh, rl, start=False, stop=False),
                        t.matmul(out, ll, rh, start=False, stop=True))[-1],
                        waits=(w0 if h == 0 else []), inc=1)
                return bl.cnt["t"]

            def emit_softmax(wi, c_t_sc):
                bankS = ps[6 + wi % 2]
                a32 = a32p[wi % 2]
                c_v_cp = bl.do("v", lambda v, bankS=bankS: v.tensor_copy(
                    s32, bankS[0:64, 0:512]), waits=[("t", c_t_sc)], inc=1)
                bank_free[6 + wi % 2] = [("v", c_v_cp)]
                bl.drain("v")
                bl.do("v", lambda v: v.reduce_max(
                    mx.rearrange("p (h o) -> p h o", o=1),
                    s32.rearrange("p (h c) -> p h c", c=64), axis=AX.X), inc=0)
                bl.drain("v")
                for h in range(8):
                    bl.do("v", lambda v, h=h: v.tensor_scalar(
                        s32[:, h * 64:(h + 1) * 64], s32[:, h * 64:(h + 1) * 64],
                        mx[:, h:h + 1], None, ALU.subtract), inc=0)
                bl.drain("v")
                c_v_sub = bl.do("v", lambda v: v.tensor_copy(
                    af[0:1, O_T2 + 56:O_T2 + 57], af[0:1, O_T2 + 56:O_T2 + 57]),
                    inc=1)
                c_s_exp = bl.do("s", lambda s: s.activation(e32, s32, AF.Exp),
                                waits=[("v", c_v_sub)], inc=1)
                bl.do("v", lambda v: v.reduce_sum(
                    sm.rearrange("p (h o) -> p h o", o=1),
                    e32.rearrange("p (h c) -> p h c", c=64), axis=AX.X),
                    waits=[("s", c_s_exp)], inc=0)
                bl.drain("v")
                bl.do("v", lambda v: v.reciprocal(sm, sm), inc=0)
                bl.drain("v")
                for h in range(8):
                    bl.do("v", lambda v, h=h, a32=a32: v.tensor_scalar(
                        a32[:, h * 64:(h + 1) * 64], e32[:, h * 64:(h + 1) * 64],
                        sm[:, h:h + 1], None, ALU.mult), inc=0)
                bl.drain("v")
                return bl.do("v", lambda v: v.tensor_copy(
                    af[0:1, O_T2 + 56:O_T2 + 57], af[0:1, O_T2 + 56:O_T2 + 57]),
                    inc=1)

            def emit_tail(wi, c_v_a32):
                a32 = a32p[wi % 2]
                bankT = ps[4 + wi % 2]
                wT = [("v", c_v_a32)] + bank_free[4 + wi % 2]
                for h in range(8):
                    bl.do("t", lambda t, h=h, a32=a32, bankT=bankT: t.transpose(
                        bankT[0:64, h * 64:(h + 1) * 64],
                        a32[:, h * 64:(h + 1) * 64],
                        ident[0:64, 0:64]), waits=(wT if h == 0 else []), inc=1)
                c_t_tr = bl.cnt["t"]
                c_s_at = bl.do("s", lambda s, bankT=bankT: s.activation(
                    ab[0:64, O_AT_H:O_AT_H + 512], bankT[0:64, 0:512], AF.Copy),
                    waits=[("t", c_t_tr)], inc=1)
                c_v_at = bl.do("v", lambda v, bankT=bankT: v.tensor_tensor(
                    ab[0:64, O_AT_L:O_AT_L + 512], bankT[0:64, 0:512],
                    ab[0:64, O_AT_H:O_AT_H + 512], ALU.subtract),
                    waits=[("s", c_s_at)], inc=1)
                bank_free[4 + wi % 2] = [("v", c_v_at)]
                bankO = ps[2 + wi % 2]
                wO = [("v", c_v_at)] + bank_free[2 + wi % 2]
                for h in range(8):
                    lh = ab[0:64, O_V_H + wi * 1024 + h * 128:O_V_H + wi * 1024 + (h + 1) * 128]
                    ll = ab[0:64, O_V_L + wi * 1024 + h * 128:O_V_L + wi * 1024 + (h + 1) * 128]
                    rh = ab[0:64, O_AT_H + h * 64:O_AT_H + (h + 1) * 64]
                    rl = ab[0:64, O_AT_L + h * 64:O_AT_L + (h + 1) * 64]
                    out = bankO[:, h * 64:(h + 1) * 64]
                    bl.do("t", lambda t, lh=lh, ll=ll, rh=rh, rl=rl, out=out: (
                        t.matmul(out, lh, rh, start=True, stop=False),
                        t.matmul(out, lh, rl, start=False, stop=False),
                        t.matmul(out, ll, rh, start=False, stop=True))[-1],
                        waits=(wO if h == 0 else []), inc=1)
                c_t_o = bl.cnt["t"]
                c_s_o = bl.do("s", lambda s, wi=wi, bankO=bankO: s.activation(
                    A(O_O_H, 4096).rearrange("p (f c) -> p f c", c=512)[:, :, wi * 64:wi * 64 + 64],
                    bankO[:, 0:512].rearrange("p (f c) -> p f c", c=64), AF.Copy),
                    waits=[("t", c_t_o)], inc=1)
                c_v_o = bl.do("v", lambda v, wi=wi, bankO=bankO: v.tensor_tensor(
                    A(O_O_L, 4096).rearrange("p (f c) -> p f c", c=512)[:, :, wi * 64:wi * 64 + 64],
                    bankO[:, 0:512].rearrange("p (f c) -> p f c", c=64),
                    A(O_O_H, 4096).rearrange("p (f c) -> p f c", c=512)[:, :, wi * 64:wi * 64 + 64],
                    ALU.subtract), waits=[("s", c_s_o)], inc=1)
                bank_free[2 + wi % 2] = [("v", c_v_o)]

            for wi in range(8):
                c_t_sc = emit_scores(wi)
                c_v_a32 = emit_softmax(wi, c_t_sc)
                win_state[wi] = c_v_a32
                if wi > 0:
                    emit_tail(wi - 1, win_state[wi - 1])
            emit_tail(7, win_state[7])
            c_v_owin = bl.cnt["v"]

            if sub == 0:
                c_rs = ln_stats(4, 4, lambda tc: [x_ready[tc]])
                ln_apply(4, 4, c_rs)
                ln_xln[1] = bl.cnt["v"]

            c_t_qk_done = bl.cnt["t"]
            for fi in range(8):
                bl.do("y", lambda y, fi=fi: y.dma_start(
                    out=A(O_OT_H + fi * 2048, 2048),
                    in_=woT.ap()[fi].rearrange("p s f -> p (s f)")),
                    waits=[("t", c_t_qk_done), ("gs0", GS0)], inc=("yo", 16))
            c_y_oT = bl.cnt["yo"]
            for tcl in range(4):
                tc = tc0 + tcl
                for half in range(2):
                    par = (tcl * 2 + half) % 4
                    bank = ps[2 + par]
                    w = [("yo", c_y_oT), ("v", c_v_owin)] + bank_free[2 + par]
                    first = True
                    for fi in range(8):
                        lh = A(O_O_H + fi * 512 + tcl * 128, 128)
                        ll = A(O_O_L + fi * 512 + tcl * 128, 128)
                        rh = A(O_OT_H + fi * 2048 + half * 512, 512)
                        rl = A(O_OT_H + fi * 2048 + 1024 + half * 512, 512)
                        bl.do("t", lambda t, lh=lh, ll=ll, rh=rh, rl=rl,
                              first=first, bank=bank:
                              mm3(t, bank[:], lh, ll, rh, rl, first, False),
                              waits=(w if first else []), inc=1)
                        first = False
                    c_t = bl.do("t", lambda t, bank=bank, half=half:
                                sel_bias(t, bank[:], 8, half * 512, 512, True), inc=1)
                    c_v = bl.do("v", lambda v, bank=bank, tc=tc, half=half:
                                v.scalar_tensor_tensor(
                                    Af(O_X1 + tc * 1024 + half * 512, 512), bank[:],
                                    2.0, Af(O_X1 + tc * 1024 + half * 512, 512),
                                    ALU.mult, ALU.add), waits=[("t", c_t)], inc=1)
                    bank_free[2 + par] = [("v", c_v)]

            if sub == 0:
                ln_stats(0, 4, [("v", bl.cnt["v"]), ("t", bl.cnt["t"])])

        c_rs = ln_stats(4, 4, [("v", bl.cnt["v"]), ("t", bl.cnt["t"])])
        ln_apply(0, 8, c_rs)
        c_v_xln = bl.cnt["v"]

        for tcl in range(8):
            par = tcl % 2
            bank = ps[2 + par]
            w = [("v", c_v_xln)] + bank_free[2 + par]
            first = True
            for dc in range(8):
                lh = A(O_XLN_H + dc * 1024 + tcl * 128, 128)
                ll = A(O_XLN_L + dc * 1024 + tcl * 128, 128)
                bl.do("t", lambda t, lh=lh, ll=ll, dc=dc, first=first, bank=bank:
                      mm3(t, bank[:, 0:E_R], lh, ll, rwt[:, dc, 0, :],
                          rwt[:, dc, 1, :], first, False),
                      waits=(w if first else []), inc=1)
                first = False
            c_t = bl.do("t", lambda t, bank=bank:
                        sel_bias(t, bank[:, 0:E_R], 10, 0, E_R, True), inc=1)
            c_v = bl.do("v", lambda v, bank=bank, tcl=tcl: v.tensor_copy(
                Af(O_GATE + tcl * 8, 8), bank[:, 0:E_R]), waits=[("t", c_t)], inc=1)
            bank_free[2 + par] = [("v", c_v)]
        bl.drain("v")
        T0 = O_T2
        for tcl in range(8):
            g8 = Af(O_GATE + tcl * 8, 8)
            cb8 = Af(O_COMB + tcl * 8, 8)
            m1, m2, r1, e2v, p1, p2 = (Af(T0, 1), Af(T0 + 1, 1), Af(T0 + 2, 1),
                                       Af(T0 + 3, 1), Af(T0 + 4, 1), Af(T0 + 5, 1))
            is1, s1t = Af(T0 + 8, 8), Af(T0 + 16, 8)
            mk1, mk2, g2t = Af(T0 + 24, 8), Af(T0 + 32, 8), Af(T0 + 40, 8)
            for step in [
                lambda v, a=m1, b=g8: v.reduce_max(a, b, axis=AX.X),
                lambda v, a=is1, b=g8, c=m1: v.tensor_scalar(a, b, c, None, ALU.is_ge),
                lambda v, a=s1t, b=is1: v.tensor_mul(a, b, iota[:]),
                lambda v, a=r1, b=s1t: v.reduce_max(a, b, axis=AX.X),
                lambda v, a=mk1, b=s1t, c=r1: v.tensor_scalar(a, b, c, None, ALU.is_equal),
                lambda v, a=g2t, b=mk1, c=g8: v.scalar_tensor_tensor(
                    a, b, -1e30, c, ALU.mult, ALU.add),
                lambda v, a=m2, b=g2t: v.reduce_max(a, b, axis=AX.X),
                lambda v, a=is1, b=g2t, c=m2: v.tensor_scalar(a, b, c, None, ALU.is_ge),
                lambda v, a=s1t, b=is1: v.tensor_mul(a, b, iota[:]),
                lambda v, a=r1, b=s1t: v.reduce_max(a, b, axis=AX.X),
                lambda v, a=mk2, b=s1t, c=r1: v.tensor_scalar(a, b, c, None, ALU.is_equal),
            ]:
                bl.do("v", step, inc=0)
                bl.drain("v")
            c_v_mm = bl.do("v", lambda v, a=e2v, b=m2, c=m1: v.tensor_tensor(
                a, b, c, ALU.subtract), inc=1)
            c_s_e2 = bl.do("s", lambda s, a=e2v: s.activation(a, a, AF.Exp),
                           waits=[("v", c_v_mm)], inc=1)
            for step in [
                lambda v, a=p1, b=e2v: v.tensor_scalar(a, b, 1.0, None, ALU.add),
                lambda v, a=p1: v.reciprocal(a, a),
                lambda v, a=p2, b=e2v, c=p1: v.tensor_mul(a, b, c),
                lambda v, a=cb8, b=mk1, c=p1: v.tensor_scalar(a, b, c, None, ALU.mult),
                lambda v, a=cb8, b=mk2, c=p2: v.scalar_tensor_tensor(
                    a, b, c, a, ALU.mult, ALU.add),
            ]:
                bl.do("v", step, waits=[("s", c_s_e2)], inc=0)
                bl.drain("v")
            bl.do("v", lambda v: v.tensor_copy(
                af[0:1, O_T2 + 56:O_T2 + 57], af[0:1, O_T2 + 56:O_T2 + 57]), inc=1)
        c_v_comb = bl.cnt["v"]

        for e in range(NEXP):
            for hc in range(16):
                blkpar = hc % 2
                sl = ws_slot(blkpar)
                c_y1 = bl.do("y", lambda y, e=e, hc=hc, sl=sl: y.dma_start(
                    out=A(sl, 2048),
                    in_=w1s.ap()[e, hc].rearrange("p s dc c -> p (s dc c)")),
                    waits=ws_free[blkpar] + [("ys0", PREP_YS0), ("ys1", PREP_YS1)],
                    inc=("y%d" % blkpar, 16))
                for half in range(2):
                    k = hc * 2 + half
                    bank = ps[k % 4]
                    w = [("y%d" % blkpar, c_y1), ("v", c_v_comb), ("v", c_v_xln)] + bank_free[k % 4]
                    first = True
                    for dc in range(8):
                        lh = A(sl + dc * 128, 128)
                        ll = A(sl + 1024 + dc * 128, 128)
                        rh = A(O_XLN_H + dc * 1024 + half * 512, 512)
                        rl = A(O_XLN_L + dc * 1024 + half * 512, 512)
                        c_t = bl.do("t", lambda t, lh=lh, ll=ll, rh=rh, rl=rl,
                                    first=first, last=(dc == 7), bank=bank:
                                    mm3(t, bank[:], lh, ll, rh, rl, first, last),
                                    waits=(w if first else []), inc=1)
                        first = False
                    if half == 1:
                        ws_free[blkpar] = [("t", c_t)]
                    stg = Af(O_G32 + half * 512, 512)
                    bcol = b1s[:, e * 16 + hc:e * 16 + hc + 1]
                    c_s1 = bl.do("s", lambda s, bank=bank, stg=stg, bcol=bcol:
                                 s.activation(stg, bank[:], AF.Gelu, bias=bcol),
                                 waits=[("t", c_t), ("v", bl.cnt["v"])], inc=1)
                    bank_free[k % 4] = [("s", c_s1)]
                    o_h = O_H_H + hc * 1024 + half * 512
                    o_l = O_H_L + hc * 1024 + half * 512
                    bl.do("v", lambda v, stg=stg, o_h=o_h: v.tensor_copy(
                        A(o_h, 512), stg), waits=[("s", c_s1)], inc=0)
                    bl.drain("v")
                    bl.do("v", lambda v, stg=stg, o_h=o_h, o_l=o_l: v.tensor_tensor(
                        A(o_l, 512), stg, A(o_h, 512), ALU.subtract), inc=1)
            c_v_h = bl.cnt["v"]
            for th in range(2):
                grp_bank = {}
                for gi in range(8):
                    grp_bank[gi] = ps[gi]
                last_e = (e == NEXP - 1)
                for hc in range(16):
                    blkpar = hc % 2
                    sl = ws_slot(blkpar)
                    c_y2 = bl.do("y", lambda y, e=e, hc=hc, sl=sl: y.dma_start(
                        out=A(sl, 2048),
                        in_=w2s.ap()[e, hc].rearrange("p s f -> p (s f)")),
                        waits=ws_free[blkpar] + [("yw0", PREP_YW0), ("yw1", PREP_YW1)],
                        inc=("y%d" % blkpar, 16))
                    for gi in range(8):
                        tcl = th * 4 + gi // 2
                        dhalf = gi % 2
                        bank = grp_bank[gi]
                        lh = A(O_H_H + hc * 1024 + tcl * 128, 128)
                        ll = A(O_H_L + hc * 1024 + tcl * 128, 128)
                        rh = A(sl + dhalf * 512, 512)
                        rl = A(sl + 1024 + dhalf * 512, 512)
                        w = [("y%d" % blkpar, c_y2), ("v", c_v_h)]
                        if hc == 0:
                            w += bank_free[gi]
                        stop_here = last_e and hc == 15
                        c_t = bl.do("t", lambda t, lh=lh, ll=ll, rh=rh, rl=rl,
                                    hc=hc, bank=bank, stop_here=stop_here:
                                    mm3(t, bank[:], lh, ll, rh, rl, hc == 0,
                                        stop_here),
                                    waits=(w if (hc == 0 or gi == 0) else []), inc=1)
                    ws_free[blkpar] = [("t", c_t)]
                for gi in range(8):
                    tcl = th * 4 + gi // 2
                    dhalf = gi % 2
                    bank = grp_bank[gi]
                    if e < E_R:
                        c_t = bl.do("t", lambda t, bank=bank, e=e, dhalf=dhalf:
                                    sel_bias(t, bank[:], e, dhalf * 512, 512, True),
                                    inc=1)
                    elif e == E_R:
                        c_t = bl.do("t", lambda t, bank=bank, dhalf=dhalf: (
                            sel_bias(t, bank[:], 9, dhalf * 512, 512, False),
                            sel_bias(t, bank[:], 12, dhalf * 512, 512, True))[-1],
                            inc=1)
                    else:
                        c_t = bl.cnt["t"]
                    x1s = Af(O_X1 + tcl * 1024 + dhalf * 512, 512)
                    if e < E_R:
                        wcol = Af(O_COMB + tcl * 8 + e, 1)
                        c_v = bl.do("v", lambda v, bank=bank, x1s=x1s, wcol=wcol:
                                    v.scalar_tensor_tensor(x1s, bank[:], wcol, x1s,
                                                           ALU.mult, ALU.add),
                                    waits=[("t", c_t)], inc=1)
                    else:
                        c_v = bl.do("v", lambda v, bank=bank, x1s=x1s:
                                    v.scalar_tensor_tensor(x1s, bank[:], 0.5, x1s,
                                                           ALU.mult, ALU.add),
                                    waits=[("t", c_t)], inc=1)
                    bank_free[gi] = [("v", c_v)]
                    if e == NEXP - 1 and dhalf == 1:
                        c_st = chunk_store(sq, hf, tcl, waits=[("v", c_v)])
                        if ri + 1 < len(ranges):
                            nsq, nhf = ranges[ri + 1]
                            chunk_load(nsq, nhf, tcl,
                                       waits=[("xc%d" % tcl, c_st)])

        x1_free = []

    bl.do("g", lambda g: g.nop(),
          waits=[("gc", bl.cnt["gc"]), ("gp0", bl.cnt["gp0"]),
                 ("gp1", bl.cnt["gp1"]), ("gp2", bl.cnt["gp2"]),
                 ("gs0", bl.cnt["gs0"]),
                 ("gs1", bl.cnt["gs1"])] +
                [("xc%d" % i, bl.cnt["xc%d" % i]) for i in range(8)], inc=0)
    bl.do("y", lambda y: y.nop(),
          waits=[("y0", bl.cnt["y0"]), ("y1", bl.cnt["y1"]), ("yo", bl.cnt["yo"]),
                 ("ys0", bl.cnt["ys0"]), ("ys1", bl.cnt["ys1"]),
                 ("yw0", bl.cnt["yw0"]), ("yw1", bl.cnt["yw1"])], inc=0)

    with nc.Block() as block:
        def run(e):
            def body(eng):
                for (waits, fn, inc) in bl.ops[e]:
                    for (se, pv) in waits:
                        eng.wait_ge(sem[se], pv)
                    ins = fn(eng)
                    if inc is not None:
                        ins.then_inc(sem[inc[0]], inc[1])
            return body

        block.tensor(run("t"))
        block.vector(run("v"))
        block.scalar(run("s"))
        block.sync(run("y"))
        block.gpsimd(run("g"))

    es.close()
    return nc


def _consts():
    identv = np.eye(128, dtype=np.float32)
    selv = np.zeros((16, 16 * 128), dtype=ml_dtypes.bfloat16)
    for r in range(16):
        selv[r, r * 128:(r + 1) * 128] = 1.0
    iotav = np.tile((float(E_R) - np.arange(E_R)).astype(np.float32), (128, 1))
    return identv, selv, iotav


_NC_CACHE = None


def kernel(L0_attn_x, L0_attn_in_proj_w, L0_attn_in_proj_b, L0_attn_out_w,
           L0_attn_out_b, L0_ffn_router_w, L0_ffn_router_b, L0_ffn_w1, L0_ffn_b1,
           L0_ffn_w2, L0_ffn_b2, L0_ffn_sw1, L0_ffn_sb1, L0_ffn_sw2, L0_ffn_sb2):
    global _NC_CACHE
    if _NC_CACHE is None:
        _NC_CACHE = build_nc()
    nc = _NC_CACHE
    identv, selv, iotav = _consts()
    x = np.ascontiguousarray(np.asarray(L0_attn_x, dtype=np.float32))
    common = dict(
        in_proj_w=np.ascontiguousarray(L0_attn_in_proj_w, dtype=np.float32),
        in_proj_b=np.ascontiguousarray(L0_attn_in_proj_b, dtype=np.float32),
        out_w=np.ascontiguousarray(L0_attn_out_w, dtype=np.float32),
        out_b=np.ascontiguousarray(L0_attn_out_b, dtype=np.float32),
        router_w=np.ascontiguousarray(L0_ffn_router_w, dtype=np.float32),
        router_b=np.ascontiguousarray(L0_ffn_router_b, dtype=np.float32),
        w1=np.ascontiguousarray(L0_ffn_w1, dtype=np.float32),
        b1=np.ascontiguousarray(L0_ffn_b1, dtype=np.float32),
        w2=np.ascontiguousarray(L0_ffn_w2, dtype=np.float32),
        b2=np.ascontiguousarray(L0_ffn_b2, dtype=np.float32),
        sw1=np.ascontiguousarray(L0_ffn_sw1, dtype=np.float32),
        sb1=np.ascontiguousarray(L0_ffn_sb1, dtype=np.float32),
        sw2=np.ascontiguousarray(L0_ffn_sw2, dtype=np.float32),
        sb2=np.ascontiguousarray(L0_ffn_sb2, dtype=np.float32),
        identc=identv, selc=selv, iotac=iotav,
    )
    in_maps = []
    for c in range(N_CORES):
        m = dict(common)
        m["x"] = np.ascontiguousarray(x[c * SEQ_PER_CORE:(c + 1) * SEQ_PER_CORE])
        in_maps.append(m)
    res = run_bass_kernel_spmd(nc, in_maps, core_ids=list(range(N_CORES)))
    return np.concatenate([r["yo"] for r in res.results], axis=0)
